# version 26
# baseline (speedup 1.0000x reference)
import numpy as np
import ml_dtypes
import concourse.bass as bass
import concourse.mybir as mybir
from concourse.bass_utils import run_bass_kernel_spmd
from contextlib import ExitStack

F32 = mybir.dt.float32
BF16 = mybir.dt.bfloat16
AF = mybir.ActivationFunctionType
ALU = mybir.AluOpType
AX = mybir.AxisListType

B, S, D = 4, 4096, 1024
H, DH, DHR = 16, 64, 32
DLAT = 512
E, HID = 8, 4096
NTOK, NT, NST = 2048, 16, 8
RS96 = float(1.0 / np.sqrt(np.float32(96.0)))
CLIP_HI = float(np.exp(np.float32(80.0) * np.float32(RS96)))
CLIP_LO = float(np.exp(np.float32(-80.0) * np.float32(RS96)))
LN_EPS = 1e-5

ENGS = ("T", "V", "S", "Y", "G")


class Plan:

    def __init__(self):
        self.ops = {e: [] for e in ENGS}
        self.cnt = {}
        self.hwm = {e: {} for e in ENGS}

    def sem(self, name):
        self.cnt[name] = 0
        for e in ENGS:
            self.hwm[e][name] = 0

    def op(self, eng, fn, inc=None, waits=()):
        ws = []
        for (s, v) in waits:
            if v > self.hwm[eng][s]:
                ws.append((s, v))
                self.hwm[eng][s] = v
        if inc is not None:
            s, k = inc
            self.cnt[s] += k
        if fn is not None or inc is not None or ws:
            self.ops[eng].append((fn, inc, ws))
        return self.cnt[inc[0]] if inc is not None else None

    def snap(self):
        return dict(self.cnt)

    def barrier(self):
        for s in ("yc", "yx0", "yx1", "ys0", "ys1"):
            self.op("Y", None, waits=[(s, self.cnt[s])])
        for s in ("ga", "gq", "gw1", "gw2"):
            self.op("G", None, waits=[(s, self.cnt[s])])
        for e in ENGS:
            self.op(e, lambda eng: eng.nop(), inc=("bar", 1))
        tgt = self.cnt["bar"]
        for e in ENGS:
            self.op(e, None, waits=[("bar", tgt)])

    def render(self, blk, sems):
        def mk(key):
            ops = self.ops[key]

            def body(eng):
                for (fn, inc, ws) in ops:
                    for (s, v) in ws:
                        eng.wait_ge(sems[s], v)
                    if fn is None:
                        if inc is not None:
                            eng.nop().then_inc(sems[inc[0]], inc[1])
                    else:
                        inst = fn(eng)
                        if inc is not None:
                            inst.then_inc(sems[inc[0]], inc[1])
            return body

        blk.tensor(mk("T"))
        blk.vector(mk("V"))
        blk.scalar(mk("S"))
        blk.sync(mk("Y"))
        blk.gpsimd(mk("G"))


def build_program(debug=False):
    nc = bass.Bass("TRN2", target_bir_lowering=False, debug=False)
    pl = Plan()
    DMA_SEMS = ("yc", "yx0", "yx1", "ys0", "ys1", "ga", "gq", "gw1", "gw2")
    for s in ("bar", "tS", "vS", "sS") + DMA_SEMS:
        pl.sem(s)

    d = {}

    def din(name, shape, dt=F32):
        d[name] = nc.dram_tensor(name, list(shape), dt, kind="ExternalInput").ap()
        return d[name]

    xf_d = din("xf", (S, D))
    xo_d = din("xo", (NTOK, D))
    cosk_d = din("cosk", (16, S), BF16)
    sink_d = din("sink", (16, S), BF16)
    cosq_d = din("cosq", (128, NTOK), BF16)
    sinq_d = din("sinq", (128, NTOK), BF16)
    masks_d = din("masks", (128, 4 * 256), BF16)
    i128_d = din("i128", (128, 128), BF16)
    ones_d = din("ones1", (65, 128), BF16)
    wdq_d = din("wdq_t", (D, DLAT))
    wdkv_d = din("wdkv_t", (D, DLAT))
    wkr_d = din("wkr_t", (D, DHR))
    wuq_d = din("wuq_t", (DLAT, H * DH))
    wqr_d = din("wqr_t", (DLAT, H * DHR))
    wuk_d = din("wuk_t", (DLAT, H * DH))
    wuv_d = din("wuv_t", (DLAT, H * DH))
    wo_d = din("wo_t", (D, D))
    wg_d = din("wg", (D, E))
    bg_d = din("bg2", (E, 1))
    b1_d = din("b1r", (128, E * 32))
    b2_d = din("b2r", (E, D))
    w1_d = din("w1", (E, D, HID))
    w2_d = din("w2", (E, HID, D))

    out_d = nc.dram_tensor("out", [NTOK, D], F32, kind="ExternalOutput").ap()
    dbg = {}
    if debug:
        for nm, shape, dt in [
            ("dbg_xn", (128, 8 * 2048), BF16), ("dbg_ckv", (128, 4 * 4096), BF16),
            ("dbg_kr", (48, 4096), BF16), ("dbg_cq", (128, 4 * 2048), BF16),
            ("dbg_g", (128, 128), F32), ("dbg_lgt", (128, 128), F32),
            ("dbg_attn", (128, 8 * 2048), BF16), ("dbg_acc4", (128, 16 * 1024), BF16),
            ("dbg_acc5", (128, 16 * 1024), BF16), ("dbg_qh", (96, 2048), BF16),
            ("dbg_k", (96, 4096), BF16), ("dbg_v", (128, 32 * 260), BF16),
        ]:
            dbg[nm] = nc.dram_tensor(nm, list(shape), dt, kind="ExternalOutput").ap()

    def dma_batch(eng, sem, pairs, waits=()):
        c = None
        first = [(sem, pl.cnt[sem])] + list(waits)
        for i, (dst, srcap) in enumerate(pairs):
            c = pl.op(eng, lambda e, o=dst, s=srcap: e.dma_start(out=o, in_=s),
                      inc=(sem, 16), waits=first if i == 0 else ())
        return c

    def dbg_dump(nm, ap, dst=None):
        if debug:
            pl.op("Y", None, waits=[("vS", pl.cnt["vS"]), ("sS", pl.cnt["sS"]), ("tS", pl.cnt["tS"])])
            dma_batch("Y", "yc", [(dbg[nm] if dst is None else dst, ap)])
            pl.barrier()

    glb = ExitStack()
    sb = lambda name, shape, dt: glb.enter_context(nc.sbuf_tensor(name, list(shape), dt))

    xn_own = sb("xn_own", (128, 8 * 2048), BF16)
    g_sb = sb("g_sb", (128, 16 * 8), F32)
    g_T = sb("g_T", (8, 2048), BF16)
    b2_sb = sb("b2_sb", (8, 1024), BF16)
    b1_sb = sb("b1_sb", (128, E * 32), F32)
    ones1 = sb("ones1s", (65, 128), BF16)
    eps_col = sb("eps_col", (128, 1), F32)
    kr_st = ExitStack()
    kr = kr_st.enter_context(nc.sbuf_tensor("kr", [48, 4096], BF16))
    kv_st = ExitStack()
    ckv = kv_st.enter_context(nc.sbuf_tensor("ckv", [128, 4 * 4096], BF16, side="right"))
    cq = kv_st.enter_context(nc.sbuf_tensor("cq", [128, 4 * 2048], BF16, side="right"))
    cc_st = ExitStack()

    pl.op("V", lambda e: e.memset(eps_col[:], LN_EPS), inc=("vS", 1))
    dma_batch("Y", "yc", [(ones1[:], ones_d), (b1_sb[:], b1_d)])
    dma_batch("G", "ga", [(b2_sb[:], b2_d)])

    with ExitStack() as P1:
        sb1 = lambda name, shape, dt: P1.enter_context(nc.sbuf_tensor(name, list(shape), dt, side="right"))
        ps1 = lambda name, shape, dt=F32: P1.enter_context(nc.psum_tensor(name, list(shape), dt))
        x_st = [sb1(f"x_st{i}", (128, 4096), F32) for i in range(2)]
        xn_tm = [sb1(f"xn_tm{i}", (128, 4096), BF16) for i in range(2)]
        xn_fm = [sb1(f"xn_fm{i}", (128, 8 * 512), BF16) for i in range(2)]
        cosk = sb1("cosk_s", (16, S), BF16)
        sink = sb1("sink_s", (16, S), BF16)
        wdkv = sb1("wdkv_s", (128, 8 * 512), BF16)
        wkr = sb1("wkr_s", (128, 8 * 32), BF16)
        wdq = sb1("wdq_s", (128, 8 * 512), BF16)
        wg = sb1("wg_s", (128, 8 * 8), BF16)
        lgbf = sb1("lgbf", (8, 2048), BF16)
        sq_s = sb1("sq_s", (128, 1024), BF16)
        mcol = [sb1(f"mcol{i}", (128, 16), F32) for i in range(2)]
        t1_s = sb1("t1_s", (16, 512), BF16)
        t2_s = sb1("t2_s", (16, 512), BF16)
        i128 = sb1("i128s", (128, 128), BF16)
        bg_sb = sb1("bg_sb", (8, 1), F32)
        lgt_sb = sb1("lgt_sb", (128, 16 * 8), F32)
        g_bf = sb1("g_bf", (128, 16 * 8), BF16)
        c_i128 = None
        ps_tr = [ps1(f"ps_tr{i}", (128, 512), BF16) for i in range(2)]
        ps_pr = [ps1(f"ps_pr{i}", (128, 512)) for i in range(4)]
        ps_kra = ps1("ps_kra", (16, 512))
        ps_krb = ps1("ps_krb", (16, 512))

        c_yc1 = dma_batch("Y", "yc", [(i128[:], i128_d), (bg_sb[:], bg_d),
                                      (cosk[:], cosk_d), (sink[:], sink_d)])
        c_ga1 = dma_batch("G", "ga",
                          [(wdkv[:, dc * 512:(dc + 1) * 512], wdkv_d[dc * 128:(dc + 1) * 128, :]) for dc in range(8)]
                          + [(wkr[:, dc * 32:(dc + 1) * 32], wkr_d[dc * 128:(dc + 1) * 128, :]) for dc in range(8)]
                          + [(wdq[:, dc * 512:(dc + 1) * 512], wdq_d[dc * 128:(dc + 1) * 128, :]) for dc in range(8)]
                          + [(wg[:, dc * 8:(dc + 1) * 8], wg_d[dc * 128:(dc + 1) * 128, :]) for dc in range(8)])
        c_wdkv = c_wkr = c_wdq = c_wg = c_ga1

        stage_free = [None, None]
        tr_evacs = []

        def emit_means(par, row0, xdram):
            w0 = []
            if stage_free[par] is not None:
                sf = stage_free[par]
                w0 = [("tS", sf["tS"]), ("vS", sf["vS"]), ("sS", sf["sS"])]
            xsem = "yx%d" % par
            ldc = dma_batch("Y", xsem,
                            [(x_st[par][:, k * 1024:(k + 1) * 1024],
                              xdram[row0 + k * 128: row0 + (k + 1) * 128, :]) for k in range(4)],
                            waits=w0)
            mc = mcol[par]
            cols = [(x_st[par][:, k * 1024:(k + 1) * 1024],
                     mc[:, k * 4 + 0: k * 4 + 1], mc[:, k * 4 + 1: k * 4 + 2],
                     mc[:, k * 4 + 2: k * 4 + 3], mc[:, k * 4 + 3: k * 4 + 4]) for k in range(4)]
            mv = []
            for k in range(4):
                xs, c_s, c_nm, c_sd, c_rs = cols[k]
                mv.append(pl.op("V", lambda e, i=xs, o=c_s: e.reduce_sum(o, i, axis=AX.X),
                                inc=("vS", 1), waits=[(xsem, ldc)]))
            return cols, mv

        def ln_round(par, pre, dst_fn, consumer):
            cols, mv = pre
            sqv = [None] * 4
            for k in range(4):
                xs, c_s, c_nm, c_sd, c_rs = cols[k]
                sv = pl.op("S", lambda e, i=c_s, o=c_nm: e.mul(o, i, -1.0 / 1024.0),
                           inc=("sS", 1), waits=[("vS", mv[k])])
                sqv[k] = pl.op("S", lambda e, i=xs, o=sq_s[:], b=c_nm, a=c_sd:
                               e.activation(o, i, AF.Square, bias=b, accum_out=a),
                               inc=("sS", 1), waits=[("sS", sv)])
            sqr = []
            for k in range(4):
                xs, c_s, c_nm, c_sd, c_rs = cols[k]
                sqr.append(pl.op("S", lambda e, i=c_sd, o=c_sd:
                                 e.activation(o, i, AF.Sqrt, scale=1.0 / 1024.0, bias=eps_col[:]),
                                 inc=("sS", 1), waits=[("sS", sqv[k])]))
            rcv = []
            for k in range(4):
                xs, c_s, c_nm, c_sd, c_rs = cols[k]
                rcv.append(pl.op("V", lambda e, i=c_sd, o=c_rs: e.reciprocal(o, i),
                                 inc=("vS", 1), waits=[("sS", sqr[k])]))
            for k in range(4):
                xs, c_s, c_nm, c_sd, c_rs = cols[k]
                vv = pl.op("V", lambda e, i=xs, o=xn_tm[par][:, k * 1024:(k + 1) * 1024], m=c_nm, r=c_rs:
                           e.tensor_scalar(o, i, m, r, ALU.add, ALU.mult), inc=("vS", 1),
                           waits=[("vS", rcv[k])])
                evh = []
                for h in range(2):
                    rot = len(tr_evacs) % 2
                    gate = tr_evacs[-2] if len(tr_evacs) >= 2 else ("sS", 0)
                    for j in range(4):
                        dc = h * 4 + j
                        w0 = []
                        if j == 0:
                            w0 = [gate]
                            if h == 0:
                                w0 += [("vS", vv), ("yc", c_yc1)]
                        tv = pl.op("T", lambda e, i=xn_tm[par][:, k * 1024 + dc * 128: k * 1024 + (dc + 1) * 128],
                                   o=ps_tr[rot][:, j * 128:(j + 1) * 128]: e.transpose(o, i, i128[:]),
                                   inc=("tS", 1) if j == 3 else None,
                                   waits=w0)
                    ev = pl.op("S", lambda e, i=ps_tr[rot][:].rearrange("p (c w) -> p c w", c=4),
                               o=dst_fn(k, h): e.copy(o, i),
                               inc=("sS", 1), waits=[("tS", tv)])
                    evh.append(ev)
                    tr_evacs.append(("sS", ev))
                consumer(k, tuple(evh))
            return tr_evacs[-1][1]

        pb_free = [0, 0, 0, 0]
        kr_free = [0]
        lg_free = [0]

        pre = emit_means(0, 0, xf_d)
        for r in range(8):
            par = r % 2
            tvk = {}

            def consumer(k, ev, par=par, r=r, tvk=tvk):
                evA, evB = ev
                for half in range(2):
                    pso = ps_kra if half == 0 else ps_krb
                    for dc in range(8):
                        w0 = []
                        if dc == 4 and half == 0:
                            w0 = [("sS", evB)]
                        if dc == 0 and half == 0:
                            w0 = [("sS", evA)]
                            if k == 0:
                                w0 += [("ga", c_wkr), ("vS", kr_free[0])]
                        tv = pl.op("T", lambda e, w=wkr[:, dc * 32 + half * 16: dc * 32 + half * 16 + 16],
                                   x=xn_fm[par][:, dc * 512 + k * 128: dc * 512 + (k + 1) * 128],
                                   o=pso[:, k * 128:(k + 1) * 128],
                                   st=(k == 0 and dc == 0), sp=(k == 3 and dc == 7):
                                   e.matmul(o, w, x, start=st, stop=sp),
                                   inc=("tS", 1) if (k == 3 and dc == 7 and half == 1) else None,
                                   waits=w0)
                    if k == 3 and half == 1:
                        tvk["tvb"] = tv
                for lc in range(4):
                    tv = None
                    for dc in range(8):
                        w0 = []
                        if dc == 0:
                            w0 = [("sS", evA)]
                            if k == 0:
                                w0 += [("ga", c_wdkv), ("sS", pb_free[lc])]
                        if dc == 4 and lc == 0:
                            w0 = [("sS", evB)]
                        tv = pl.op("T", lambda e, w=wdkv[:, dc * 512 + lc * 128: dc * 512 + (lc + 1) * 128],
                                   x=xn_fm[par][:, dc * 512 + k * 128: dc * 512 + (k + 1) * 128],
                                   o=ps_pr[lc][:, k * 128:(k + 1) * 128],
                                   st=(k == 0 and dc == 0), sp=(k == 3 and dc == 7):
                                   e.matmul(o, w, x, start=st, stop=sp),
                                   inc=("tS", 1) if (k == 3 and dc == 7) else None,
                                   waits=w0)
                    if k == 3:
                        ev2 = pl.op("S", lambda e, i=ps_pr[lc][:], o=ckv[:, lc * 4096 + r * 512: lc * 4096 + (r + 1) * 512]:
                                    e.copy(o, i), inc=("sS", 1), waits=[("tS", tv)])
                        pb_free[lc] = ev2

            ln_round(par, pre,
                     lambda k, h, par=par: xn_fm[par][:].rearrange("p (c w) -> p c w", c=8)[:, h * 4:h * 4 + 4, k * 128:(k + 1) * 128],
                     consumer)
            tvb = tvk["tvb"]
            if r < 7:
                pre = emit_means((r + 1) % 2, (r + 1) * 512, xf_d)
            else:
                pre = emit_means(0, 0, xo_d)
            ck = cosk[:, r * 512:(r + 1) * 512]
            sk = sink[:, r * 512:(r + 1) * 512]
            t1h = t1_s[0:16, 0:512]
            t2h = t2_s[0:16, 0:512]
            rc = pl.op("V", lambda e, c=ck: e.tensor_mul(t1h, ps_kra[:], c),
                       inc=("vS", 1), waits=[("tS", tvb), ("yc", c_yc1)])
            rc = pl.op("V", lambda e, s=sk: e.tensor_mul(t2h, ps_krb[:], s), inc=("vS", 1))
            rc = pl.op("V", lambda e, o=kr[0:16, r * 512:(r + 1) * 512]: e.tensor_sub(o, t1h, t2h),
                       inc=("vS", 1), waits=[("vS", rc)])
            rc = pl.op("V", lambda e, s=sk: e.tensor_mul(t1h, ps_kra[:], s),
                       inc=("vS", 1), waits=[("vS", rc)])
            rc = pl.op("V", lambda e, c=ck: e.tensor_mul(t2h, ps_krb[:], c), inc=("vS", 1))
            kf = pl.op("V", lambda e, o=kr[32:48, r * 512:(r + 1) * 512]: e.tensor_add(o, t1h, t2h),
                       inc=("vS", 1), waits=[("vS", rc)])
            kr_free[0] = kf
            stage_free[par] = pl.snap()

        for r in range(4):
            par = r % 2

            def consumer(k, ev, par=par, r=r):
                evA, evB = ev
                for lc in range(4):
                    tv = None
                    for dc in range(8):
                        w0 = []
                        if dc == 0:
                            w0 = [("sS", evA)]
                            if k == 0:
                                w0 += [("ga", c_wdq), ("sS", pb_free[lc])]
                        if dc == 4 and lc == 0:
                            w0 = [("sS", evB)]
                        tv = pl.op("T", lambda e, w=wdq[:, dc * 512 + lc * 128: dc * 512 + (lc + 1) * 128],
                                   x=xn_own[:, dc * 2048 + r * 512 + k * 128: dc * 2048 + r * 512 + (k + 1) * 128],
                                   o=ps_pr[lc][:, k * 128:(k + 1) * 128],
                                   st=(k == 0 and dc == 0), sp=(k == 3 and dc == 7):
                                   e.matmul(o, w, x, start=st, stop=sp),
                                   inc=("tS", 1) if (k == 3 and dc == 7) else None,
                                   waits=w0)
                    if k == 3:
                        ev2 = pl.op("S", lambda e, i=ps_pr[lc][:], o=cq[:, lc * 2048 + r * 512: lc * 2048 + (r + 1) * 512]:
                                    e.copy(o, i), inc=("sS", 1), waits=[("tS", tv)])
                        pb_free[lc] = ev2
                tv = None
                for dc in range(8):
                    w0 = []
                    if dc == 0:
                        w0 = [("sS", evA)]
                        if k == 0:
                            w0 += [("ga", c_wg), ("sS", lg_free[0]), ("vS", kr_free[0])]
                    tv = pl.op("T", lambda e, w=wg[:, dc * 8:(dc + 1) * 8],
                               x=xn_own[:, dc * 2048 + r * 512 + k * 128: dc * 2048 + r * 512 + (k + 1) * 128],
                               o=ps_kra[0:8, k * 128:(k + 1) * 128],
                               st=(k == 0 and dc == 0), sp=(k == 3 and dc == 7):
                               e.matmul(o, w, x, start=st, stop=sp),
                               inc=("tS", 1) if (k == 3 and dc == 7) else None,
                               waits=w0)
                if k == 3:
                    sv = pl.op("S", lambda e, i=ps_kra[0:8, :], o=lgbf[0:8, r * 512:(r + 1) * 512]:
                               e.activation(o, i, AF.Identity, bias=bg_sb[0:8, 0:1]),
                               inc=("sS", 1), waits=[("tS", tv), ("yc", c_yc1)])
                    lg_free[0] = sv

            ln_round(par, pre,
                     lambda k, h, r=r: xn_own[:].rearrange("p (c w) -> p c w", c=8)[:, h * 4:h * 4 + 4, r * 512 + k * 128: r * 512 + (k + 1) * 128],
                     consumer)
            if r < 3:
                pre = emit_means((r + 1) % 2, (r + 1) * 512, xo_d)
            stage_free[par] = pl.snap()

        lgt_ev = []
        for tj in range(NT):
            pst = ps_pr[2 + tj % 2][0:128, 0:8]
            gate = lgt_ev[-2] if len(lgt_ev) >= 2 else 0
            w0 = [("sS", lg_free[0]), ("vS", gate)]
            if len(lgt_ev) < 2:
                w0.append(("sS", pl.cnt["sS"]))
            tv = pl.op("T", lambda e, i=lgbf[0:8, tj * 128:(tj + 1) * 128], o=pst:
                       e.matmul(o, i, i128[0:8, 0:8], start=True, stop=True),
                       inc=("tS", 1), waits=w0)
            ev = pl.op("V", lambda e, o=lgt_sb[:, tj * 8:(tj + 1) * 8], i=pst: e.tensor_copy(o, i),
                       inc=("vS", 1), waits=[("tS", tv)])
            lgt_ev.append(ev)

        gsc = sb1("gsc", (128, 16 * 8), F32)
        m1s = sb1("m1s", (128, 16), F32)
        m2s = sb1("m2s", (128, 16), F32)
        dens = sb1("dens", (128, 16), F32)
        negm = sb1("negm", (128, 16), F32)
        e_s = sb1("e_s", (128, 16 * 8), F32)
        lgt3 = lgt_sb[:].rearrange("p (t e) -> p t e", t=16)
        gsc3 = gsc[:].rearrange("p (t e) -> p t e", t=16)
        vv = pl.op("V", lambda e: e.reduce_max(m1s[:].rearrange("p (t o) -> p t o", o=1), lgt3, axis=AX.X),
                   inc=("vS", 1), waits=[("vS", lgt_ev[-1])])
        sv = pl.op("S", lambda e: e.mul(negm[:], m1s[:], -1.0), inc=("sS", 1), waits=[("vS", vv)])
        for tj in range(NT):
            sv = pl.op("S", lambda e, i=lgt_sb[:, tj * 8:(tj + 1) * 8], o=e_s[:, tj * 8:(tj + 1) * 8],
                       b=negm[:, tj:tj + 1]: e.activation(o, i, AF.Exp, bias=b), inc=("sS", 1),
                       waits=[("sS", sv)])
        for tj in range(NT):
            c1 = pl.op("V", lambda e, i=lgt_sb[:, tj * 8:(tj + 1) * 8], o=gsc[:, tj * 8:(tj + 1) * 8],
                       m=m1s[:, tj:tj + 1]: e.tensor_scalar(o, i, m, None, ALU.is_ge),
                       inc=("vS", 1), waits=[("sS", sv)])
            pl.op("V", lambda e, i=gsc[:, tj * 8:(tj + 1) * 8], l=lgt_sb[:, tj * 8:(tj + 1) * 8],
                  o=gsc[:, tj * 8:(tj + 1) * 8]: e.scalar_tensor_tensor(o, i, -1e30, l, ALU.mult, ALU.add),
                  inc=("vS", 1), waits=[("vS", c1)])
        vv = pl.op("V", lambda e: e.reduce_max(m2s[:].rearrange("p (t o) -> p t o", o=1), gsc3, axis=AX.X),
                   inc=("vS", 1), waits=[("vS", pl.cnt["vS"])])
        sv = pl.op("S", lambda e: e.mul(m2s[:], m2s[:], 1.0), inc=("sS", 1), waits=[("vS", vv)])
        for tj in range(NT):
            c1 = pl.op("V", lambda e, i=lgt_sb[:, tj * 8:(tj + 1) * 8], o=gsc[:, tj * 8:(tj + 1) * 8],
                       m=m2s[:, tj:tj + 1]: e.tensor_scalar(o, i, m, None, ALU.is_ge),
                       inc=("vS", 1), waits=[("sS", sv)])
            pl.op("V", lambda e, a=e_s[:, tj * 8:(tj + 1) * 8], o=gsc[:, tj * 8:(tj + 1) * 8],
                  i=gsc[:, tj * 8:(tj + 1) * 8]:
                  e.tensor_mul(o, i, a),
                  inc=("vS", 1), waits=[("vS", c1)])
        vv = pl.op("V", lambda e: e.reduce_sum(dens[:].rearrange("p (t o) -> p t o", o=1), gsc3, axis=AX.X),
                   inc=("vS", 1), waits=[("vS", pl.cnt["vS"])])
        sv = pl.op("S", lambda e: e.mul(dens[:], dens[:], 1.0), inc=("sS", 1), waits=[("vS", vv)])
        vv = pl.op("V", lambda e: e.reciprocal(m1s[:], dens[:]), inc=("vS", 1), waits=[("sS", sv)])
        for tj in range(NT):
            vv = pl.op("V", lambda e, i=gsc[:, tj * 8:(tj + 1) * 8], o=g_sb[:, tj * 8:(tj + 1) * 8],
                       r=m1s[:, tj:tj + 1]: e.tensor_scalar(o, i, r, None, ALU.mult),
                       inc=("vS", 1), waits=[("vS", vv)])
        vv = pl.op("V", lambda e: e.tensor_copy(g_bf[:], g_sb[:]), inc=("vS", 1), waits=[("vS", vv)])
        gT_ev = []
        for tj in range(NT):
            pst = ps_pr[2 + tj % 2][0:8, 0:128]
            gate = gT_ev[-2] if len(gT_ev) >= 2 else 0
            tv = pl.op("T", lambda e, i=g_bf[:, tj * 8:(tj + 1) * 8], o=pst:
                       e.matmul(o, i, i128[:], start=True, stop=True),
                       inc=("tS", 1), waits=[("vS", vv), ("vS", gate)])
            ev = pl.op("V", lambda e, o=g_T[0:8, tj * 128:(tj + 1) * 128], i=pst: e.tensor_copy(o, i),
                       inc=("vS", 1), waits=[("tS", tv)])
            gT_ev.append(ev)

        dbg_dump("dbg_xn", xn_own[:])
        dbg_dump("dbg_ckv", ckv[:])
        dbg_dump("dbg_kr", kr[0:16, :], dst=dbg["dbg_kr"][0:16, :] if debug else None)
        dbg_dump("dbg_kr", kr[32:48, :], dst=dbg["dbg_kr"][32:48, :] if debug else None)
        dbg_dump("dbg_cq", cq[:])
        dbg_dump("dbg_g", g_sb[:])
        dbg_dump("dbg_lgt", lgt_sb[:])
        pl.barrier()

    with ExitStack() as P3:
        sb3 = lambda name, shape, dt: P3.enter_context(nc.sbuf_tensor(name, list(shape), dt, side="right"))
        ps3 = lambda name, shape: P3.enter_context(nc.psum_tensor(name, list(shape), F32))
        Kb = [sb3(f"Kb{i}", (96, 4096), BF16) for i in range(2)]
        qh = [sb3(f"qh{i}", (96, 2048), BF16) for i in range(2)]
        kpref = dma_batch("G", "gq", [(Kb[0][64:80, :], kr[0:16, :]), (Kb[0][80:96, :], kr[32:48, :]),
                                      (Kb[1][64:80, :], kr[0:16, :]), (Kb[1][80:96, :], kr[32:48, :])])
        kr_st.close()
        attn_cc = cc_st.enter_context(nc.sbuf_tensor("attn_cc", [128, 8 * 2048], BF16))
        wuq_q = [sb3(f"wuq_q{i}", (128, 4 * 256), BF16) for i in range(2)]
        wuk_q = [sb3(f"wuk_q{i}", (128, 4 * 256), BF16) for i in range(2)]
        wqr_q = [sb3(f"wqr_q{i}", (128, 4 * 128), BF16) for i in range(2)]
        wuv_q = sb3("wuv_q", (128, 4 * 256), BF16)
        cosq = sb3("cosq_s", (128, 2048), BF16)
        sinq = sb3("sinq_s", (128, 2048), BF16)
        Vq = sb3("Vq", (128, 32 * 260), BF16)
        qra = sb3("qra", (64, 2048), BF16)
        qrb = sb3("qrb", (64, 2048), BF16)
        avbf = sb3("avbf", (64, 512), BF16)
        t1q = sb3("t1q", (64, 512), F32)
        t2q = sb3("t2q", (65, 512), F32)
        rcp_sb = t1q
        rrbf = t2q[64:65, :].bitcast(BF16)[:, 0:512]
        ptA = [sb3(f"ptA{i}", (128, 1024), BF16) for i in range(3)]
        masks = sb3("masks_s", (128, 4 * 256), BF16)
        ps_sc = [ps3(f"ps_sc{i}", (128, 1024)) for i in range(2)]
        ps_out = [ps3(f"ps_out{i}", (65, 512)) for i in range(2)]
        ps_b = [ps3("ps_b1", (64, 512)), ps3("ps_b2", (64, 512))]

        ctab = dma_batch("Y", "yc", [(cosq[:], cosq_d), (sinq[:], sinq_d), (masks[:], masks_d)])

        st_state = {
            "last_qk_t": 0, "last_pv_t": 0, "last_fin_v": 0, "last_avc": [("sS", 0), ("sS", 0)], "grp": 0,
            "last_exp_s": [0, 0], "last_pv_for_pt": [0, 0, 0],
            "pbv_free": [("vS", 0), ("vS", 0)], "dq_prev": 0,
            "qk_last": {},
            "bready": {},
            "wq_loads": {},
        }

        def load_quarter_weights(hq):
            par = hq % 2
            prs = [(wuq_q[par][:, lc * 256:(lc + 1) * 256], wuq_d[lc * 128:(lc + 1) * 128, hq * 256:(hq + 1) * 256]) for lc in range(4)]
            prs += [(wuk_q[par][:, lc * 256:(lc + 1) * 256], wuk_d[lc * 128:(lc + 1) * 128, hq * 256:(hq + 1) * 256]) for lc in range(4)]
            prs += [(wqr_q[par][:, lc * 128:lc * 128 + 64], wqr_d[lc * 128:(lc + 1) * 128, hq * 64:(hq + 1) * 64]) for lc in range(4)]
            prs += [(wqr_q[par][:, lc * 128 + 64:lc * 128 + 128], wqr_d[lc * 128:(lc + 1) * 128, 256 + hq * 64:256 + (hq + 1) * 64]) for lc in range(4)]
            st_state["wq_loads"][hq] = dma_batch("G", "ga", prs,
                                                 waits=[("tS", st_state["qk_last"].get(hq * 4 - 3, 0))])

        def build_quanta(h):
            hq2, hl2 = h // 4, h % 4
            par = h % 2
            wpar = hq2 % 2
            quanta = []

            def q_fc(fc):
                def go():
                    rot = fc % 2
                    pb = ps_b[rot]
                    tv = None
                    for lc in range(4):
                        tv = pl.op("T", lambda e, w=wuq_q[wpar][:, lc * 256 + hl2 * 64: lc * 256 + (hl2 + 1) * 64],
                                   r=cq[:, lc * 2048 + fc * 512: lc * 2048 + (fc + 1) * 512],
                                   o=pb[:], st=(lc == 0), sp=(lc == 3): e.matmul(o, w, r, start=st, stop=sp),
                                   inc=("tS", 1) if lc == 3 else None,
                                   waits=([("ga", st_state["wq_loads"][hq2]), st_state["pbv_free"][rot]] if lc == 0 else []))
                    eng = "V" if fc % 2 == 0 else "S"
                    sem = "vS" if eng == "V" else "sS"
                    if eng == "V":
                        ev = pl.op("V", lambda e, i=pb[:], o=qh[par][0:64, fc * 512:(fc + 1) * 512]: e.tensor_copy(o, i),
                                   inc=("vS", 1), waits=[("tS", tv), ("tS", st_state["qk_last"].get(h - 2, 0))])
                    else:
                        ev = pl.op("S", lambda e, i=pb[:], o=qh[par][0:64, fc * 512:(fc + 1) * 512]: e.copy(o, i),
                                   inc=("sS", 1), waits=[("tS", tv), ("tS", st_state["qk_last"].get(h - 2, 0))])
                    st_state["pbv_free"][rot] = (sem, ev)
                    st_state["bready"].setdefault(h, {})["q"] = (sem, ev)
                return go

            def k_fc(fc):
                def go():
                    rot = fc % 2
                    pb = ps_b[rot]
                    tv = None
                    for lc in range(4):
                        tv = pl.op("T", lambda e, w=wuk_q[wpar][:, lc * 256 + hl2 * 64: lc * 256 + (hl2 + 1) * 64],
                                   r=ckv[:, lc * 4096 + fc * 512: lc * 4096 + (fc + 1) * 512],
                                   o=pb[:], st=(lc == 0), sp=(lc == 3): e.matmul(o, w, r, start=st, stop=sp),
                                   inc=("tS", 1) if lc == 3 else None,
                                   waits=([("ga", st_state["wq_loads"][hq2]), st_state["pbv_free"][rot]] if lc == 0 else []))
                    eng = "V" if fc % 2 == 0 else "S"
                    sem = "vS" if eng == "V" else "sS"
                    if eng == "V":
                        ev = pl.op("V", lambda e, i=pb[:], o=Kb[par][0:64, fc * 512:(fc + 1) * 512]: e.tensor_copy(o, i),
                                   inc=("vS", 1), waits=[("tS", tv), ("tS", st_state["qk_last"].get(h - 2, 0)), ("gq", kpref)])
                    else:
                        ev = pl.op("S", lambda e, i=pb[:], o=Kb[par][0:64, fc * 512:(fc + 1) * 512]: e.copy(o, i),
                                   inc=("sS", 1), waits=[("tS", tv), ("tS", st_state["qk_last"].get(h - 2, 0)), ("gq", kpref)])
                    st_state["pbv_free"][rot] = (sem, ev)
                    st_state["bready"].setdefault(h, {})["k"] = (sem, ev)
                return go

            def dist():
                def go():
                    dq = dma_batch("G", "gq", [(qh[par][64:80, :], qra[hl2 * 16:(hl2 + 1) * 16, :]),
                                               (qh[par][80:96, :], qrb[hl2 * 16:(hl2 + 1) * 16, :])],
                                   waits=[("vS", st_state["qr_ev"]), ("tS", st_state["qk_last"].get(h - 2, 0))])
                    st_state["bready"].setdefault(h, {})["dq"] = dq
                    st_state["dq_prev"] = dq
                return go

            for fc in range(4):
                quanta.append(q_fc(fc))
            for fc in range(8):
                quanta.append(k_fc(fc))
            if hl2 != 0:
                quanta.append(dist())
            return quanta

        def quarter_start(hq, filler=()):
            filler = list(filler)
            par = hq % 2
            qr_ev = 0
            for fc in range(4):
                tvb = None
                for lc in range(4):
                    pl.op("T", lambda e, w=wqr_q[par][:, lc * 128:lc * 128 + 64],
                          r=cq[:, lc * 2048 + fc * 512: lc * 2048 + (fc + 1) * 512],
                          o=ps_b[0][:], st=(lc == 0), sp=(lc == 3): e.matmul(o, w, r, start=st, stop=sp),
                          waits=([("ga", st_state["wq_loads"][hq])] + list(st_state["pbv_free"]) if lc == 0 else []))
                for lc in range(4):
                    tvb = pl.op("T", lambda e, w=wqr_q[par][:, lc * 128 + 64:lc * 128 + 128],
                                r=cq[:, lc * 2048 + fc * 512: lc * 2048 + (fc + 1) * 512],
                                o=ps_b[1][:], st=(lc == 0), sp=(lc == 3): e.matmul(o, w, r, start=st, stop=sp),
                                inc=("tS", 1) if lc == 3 else None)
                cqs = cosq[0:64, fc * 512:(fc + 1) * 512]
                sqs = sinq[0:64, fc * 512:(fc + 1) * 512]
                rc = pl.op("V", lambda e, c=cqs: e.tensor_mul(t1q[:], ps_b[0][:], c),
                           inc=("vS", 1),
                           waits=[("tS", tvb), ("yc", ctab), ("gq", st_state["dq_prev"]),
                                  ("vS", st_state["last_fin_v"])])
                rc = pl.op("V", lambda e, s=sqs: e.tensor_mul(t2q[0:64, :], ps_b[1][:], s), inc=("vS", 1))
                rc = pl.op("V", lambda e, o=qra[:, fc * 512:(fc + 1) * 512]: e.tensor_sub(o, t1q[:], t2q[0:64, :]),
                           inc=("vS", 1), waits=[("vS", rc)])
                rc = pl.op("V", lambda e, s=sqs: e.tensor_mul(t1q[:], ps_b[0][:], s),
                           inc=("vS", 1), waits=[("vS", rc)])
                rc = pl.op("V", lambda e, c=cqs: e.tensor_mul(t2q[0:64, :], ps_b[1][:], c), inc=("vS", 1))
                qr_ev = pl.op("V", lambda e, o=qrb[:, fc * 512:(fc + 1) * 512]: e.tensor_add(o, t1q[:], t2q[0:64, :]),
                              inc=("vS", 1), waits=[("vS", rc)])
                st_state["pbv_free"][0] = st_state["pbv_free"][1] = ("vS", qr_ev)
            st_state["qr_ev"] = qr_ev
            c_wuv = dma_batch("G", "ga",
                              [(wuv_q[:, lc * 256:(lc + 1) * 256], wuv_d[lc * 128:(lc + 1) * 128, hq * 256:(hq + 1) * 256]) for lc in range(4)],
                              waits=[("tS", st_state["last_pv_t"])])
            msv = pl.op("V", lambda e: e.memset(Vq[:].rearrange("p (t w) -> p t w", w=65)[:, :, 64:65], 1.0),
                        inc=("vS", 1), waits=[("tS", st_state["last_pv_t"]), ("yc", ctab)])
            vq_ev = [("vS", msv), ("vS", msv)]
            for tc in range(32):
                rot = tc % 2
                tv = None
                for lc in range(4):
                    tv = pl.op("T", lambda e, w=ckv[:, lc * 4096 + tc * 128: lc * 4096 + (tc + 1) * 128],
                               r=wuv_q[:, lc * 256:(lc + 1) * 256],
                               o=ps_sc[rot][0:128, 0:256], st=(lc == 0), sp=(lc == 3):
                               e.matmul(o, w, r, start=st, stop=sp),
                               inc=("tS", 1) if lc == 3 else None,
                               waits=([("ga", c_wuv), vq_ev[rot], ("sS", st_state["last_exp_s"][rot])] if lc == 0 else []))
                oap = Vq[:, tc * 260:(tc + 1) * 260].rearrange("p (h c) -> p h c", h=4)[:, :, 0:64]
                iap = ps_sc[rot][0:128, 0:256].rearrange("p (h c) -> p h c", h=4)
                ev = pl.op("S", lambda e, i=iap, o=oap: e.copy(o, i), inc=("sS", 1), waits=[("tS", tv)])
                vq_ev[rot] = ("sS", ev)
                if filler and tc % 3 == 2:
                    filler.pop(0)()
            st_state["vq_ready"] = max(vq_ev[0][1], vq_ev[1][1])

            hb = hq * 4
            dq = dma_batch("G", "gq", [(qh[hb % 2][64:80, :], qra[0:16, :]),
                                       (qh[hb % 2][80:96, :], qrb[0:16, :])],
                           waits=[("vS", qr_ev), ("tS", st_state["qk_last"].get(hb - 2, 0))])
            st_state["bready"].setdefault(hb, {})["dq"] = dq
            st_state["dq_prev"] = dq
            while filler:
                filler.pop(0)()

        def attention_head(h, filler):
            par = h % 2
            hl2 = h % 4
            br = st_state["bready"][h]
            fin_stages = []
            for sg in range(4):
                sts = [2 * sg, 2 * sg + 1]
                Es = [4 * st + 4 for st in sts]

                gb = st_state["grp"] % 2
                st_state["grp"] += 1

                def emit_pv(todo, rot3, cv, gb=gb):
                    tv = None
                    for item in todo:
                        i_kc, kc, i_st, st = item
                        qi = i_kc * 2 + i_st
                        w0 = [("vS", cv)]
                        if kc == 0 and i_st == 0:
                            w0.append(st_state["last_avc"][gb])
                        tv = pl.op("T", lambda e, v=Vq[:, kc * 260 + hl2 * 65: kc * 260 + hl2 * 65 + 65],
                                   p=ptA[rot3][:, qi * 256:(qi + 1) * 256],
                                   o=ps_out[gb][0:65, i_st * 256:(i_st + 1) * 256],
                                   st=(kc == 0 and i_st == 0), sp=(i_st == 1 and kc == Es[1] - 1):
                                   e.matmul(o, v, p, start=st, stop=sp),
                                   inc=("tS", 1) if item == todo[-1] else None, waits=w0)
                    st_state["last_pv_for_pt"][rot3] = tv
                    return tv

                pend = []
                for kb in range(0, Es[1], 2):
                    rot = (kb // 2) % 2
                    rot3 = (kb // 2) % 3
                    todo = []
                    for i_kc, kc in enumerate((kb, kb + 1)):
                        for i_st, st in enumerate(sts):
                            if kc < Es[i_st]:
                                todo.append((i_kc, kc, i_st, st))
                    tv = None
                    first = True
                    for item in todo:
                        i_kc, kc, i_st, st = item
                        qi = i_kc * 2 + i_st
                        w0 = []
                        if first:
                            w0 = [("sS", st_state["last_exp_s"][rot]),
                                  br["q"], br["k"],
                                  ("gq", br.get("dq", 0)), ("sS", st_state["vq_ready"])]
                            first = False
                        tv = pl.op("T", lambda e, k=Kb[par][0:96, kc * 128:(kc + 1) * 128],
                                   q=qh[par][0:96, st * 256:(st + 1) * 256],
                                   o=ps_sc[rot][0:128, qi * 256:(qi + 1) * 256]:
                                   e.matmul(o, k, q, start=True, stop=True),
                                   inc=("tS", 1) if item == todo[-1] else None, waits=w0)
                    st_state["last_qk_t"] = tv
                    if fin_stages:
                        fin_stages.pop(0)()
                    if len(pend) >= 3:
                        emit_pv(*pend.pop(0))
                    sv = pl.op("S", lambda e, i=ps_sc[rot][:], o=ptA[rot3][:]:
                               e.activation(o, i, AF.Exp, scale=RS96),
                               inc=("sS", 1), waits=[("tS", tv), ("tS", st_state["last_pv_for_pt"][rot3])])
                    st_state["last_exp_s"][rot] = sv
                    cv = pl.op("V", lambda e, i=ptA[rot3][:]:
                               e.tensor_scalar(i, i, CLIP_HI, CLIP_LO, ALU.min, ALU.max),
                               inc=("vS", 1), waits=[("sS", sv)])
                    for (i_kc, kc, i_st, st) in todo:
                        qi = i_kc * 2 + i_st
                        m = kc - 4 * st
                        if 0 <= m <= 3:
                            cv = pl.op("V", lambda e, i=ptA[rot3][:, qi * 256:(qi + 1) * 256],
                                       mk=masks[:, m * 256:(m + 1) * 256]: e.tensor_mul(i, i, mk),
                                       inc=("vS", 1), waits=[("vS", cv)])
                    pend.append((todo, rot3, cv))
                    if filler:
                        filler.pop(0)()
                for p_ in pend:
                    st_state["last_pv_t"] = emit_pv(*p_)
                def mk_fin(sts=sts, hloc=h, gb=gb):
                    lpv = st_state["last_pv_t"]
                    stages = []

                    def s_copy():
                        c1 = pl.op("S", lambda e: e.copy(rrbf[:, 0:512], ps_out[gb][64:65, :]),
                                   inc=("sS", 1), waits=[("tS", lpv)])
                        av2 = pl.op("S", lambda e: e.copy(avbf[:, 0:512], ps_out[gb][0:64, :]),
                                    inc=("sS", 1), waits=[("vS", st_state["last_fin_v"])])
                        tb = pl.op("T", lambda e: e.matmul(ps_out[gb][0:64, :], ones1[64:65, 0:64], rrbf,
                                                           start=True, stop=True),
                                   inc=("tS", 1), waits=[("sS", av2)])
                        st_state["fin_tb"] = tb
                        st_state["fin_av2"] = av2
                    stages.append(s_copy)

                    def mk_rc(c):
                        def s_rc():
                            cr = pl.op("V", lambda e, i=ps_out[gb][0:64, c * 64:(c + 1) * 64],
                                       o=rcp_sb[:, c * 64:(c + 1) * 64]: e.reciprocal(o, i),
                                       inc=("vS", 1),
                                       waits=([("tS", st_state["fin_tb"]), ("vS", st_state["last_fin_v"])] if c == 0 else []))
                            st_state["fin_cr"] = cr
                        return s_rc
                    for c in range(8):
                        stages.append(mk_rc(c))

                    def s_mul():
                        fin = pl.op("V", lambda e, a=avbf[:, 0:512], r=rcp_sb[:, 0:512],
                                    o=attn_cc[(hloc % 2) * 64:(hloc % 2) * 64 + 64,
                                              (hloc // 2) * 2048 + sts[0] * 256:(hloc // 2) * 2048 + (sts[0] + 2) * 256]:
                                    e.tensor_mul(o, a, r),
                                    inc=("vS", 1), waits=[("vS", st_state["fin_cr"]), ("sS", st_state["fin_av2"])])
                        st_state["last_fin_v"] = fin
                        st_state["last_avc"][gb] = ("vS", fin)
                    stages.append(s_mul)
                    return stages

                if sg < 3:
                    fin_stages.extend(mk_fin())
                else:
                    for s_ in mk_fin():
                        s_()
            assert not fin_stages, f"fin stages left at head {h}: {len(fin_stages)}"
            st_state["qk_last"][h] = st_state["last_qk_t"]
            while filler:
                filler.pop(0)()

        load_quarter_weights(0)
        quarter_start(0, filler=build_quanta(0))
        for h in range(16):
            if h % 4 == 0 and h > 0:
                load_quarter_weights(h // 4) if (h // 4) not in st_state["wq_loads"] else None
                quarter_start(h // 4)
            if (h + 1) // 4 != h // 4 and h + 1 < 16 and (h + 1) // 4 not in st_state["wq_loads"]:
                load_quarter_weights((h + 1) // 4)
            filler = build_quanta(h + 1) if h + 1 < 16 else []
            attention_head(h, filler)
            if debug and h == 0:
                dbg_dump("dbg_qh", qh[0][:])
                dbg_dump("dbg_k", Kb[0][:])
                dbg_dump("dbg_v", Vq[:])

        dbg_dump("dbg_attn", attn_cc[:])
        pl.barrier()
    kv_st.close()

    acc_st = ExitStack()
    acc = acc_st.enter_context(nc.sbuf_tensor("acc", [128, 16 * 1024], BF16, side="right"))
    w1buf_st = ExitStack()
    w1buf = w1buf_st.enter_context(nc.sbuf_tensor("w1buf", [128, 8 * 2048], BF16, side="right"))
    with ExitStack() as P4:
        wo = P4.enter_context(nc.sbuf_tensor("wo_s", [128, 8 * 1024], BF16, side="right"))
        ps_wo = [P4.enter_context(nc.psum_tensor(f"ps_wo{i}", [128, 1024], F32)) for i in range(2)]
        c_wo = dma_batch("G", "ga", [(wo[:, oc * 1024:(oc + 1) * 1024], wo_d[oc * 128:(oc + 1) * 128, :]) for oc in range(8)])
        w1_loads = {(0, 0): dma_batch("G", "gw1", [(w1buf[:, dc * 2048:(dc + 1) * 2048],
                                                    w1_d[0, dc * 128:(dc + 1) * 128, 0:2048]) for dc in range(8)])}
        ev_free = [0, 0]
        for tj in range(NT):
            rot = tj % 2
            tv = None
            for dh in range(2):
                for oc in range(8):
                    pl.op("T", lambda e, a=attn_cc[:, oc * 2048 + tj * 128: oc * 2048 + (tj + 1) * 128],
                          w=wo[:, oc * 1024 + dh * 512: oc * 1024 + (dh + 1) * 512],
                          o=ps_wo[rot][:, dh * 512:(dh + 1) * 512], st=(oc == 0):
                          e.matmul(o, a, w, start=st, stop=False),
                          waits=([("ga", c_wo), ("vS", ev_free[rot])] if (oc == 0 and dh == 0) else []))
                tv = pl.op("T", lambda e, g=g_T[0:8, tj * 128:(tj + 1) * 128],
                           b=b2_sb[0:8, dh * 512:(dh + 1) * 512],
                           o=ps_wo[rot][:, dh * 512:(dh + 1) * 512]:
                           e.matmul(o, g, b, start=False, stop=True),
                           inc=("tS", 1) if dh == 1 else None)
            ev = pl.op("V", lambda e, o=acc[:, tj * 1024:(tj + 1) * 1024], i=ps_wo[rot][:]: e.tensor_copy(o, i),
                       inc=("vS", 1), waits=[("tS", tv)])
            ev_free[rot] = ev
        dbg_dump("dbg_acc4", acc[:])
        pl.barrier()
    cc_st.close()

    with ExitStack() as P5:
        h_store = P5.enter_context(nc.sbuf_tensor("h_store", [128, 16 * 2048], BF16, side="right"))
        w2buf = P5.enter_context(nc.sbuf_tensor("w2buf", [128, 16 * 1024], BF16, side="right"))
        ps_h = [P5.enter_context(nc.psum_tensor(f"ps_h{i}", [128, 1024], F32)) for i in range(2)]
        ps_o = [P5.enter_context(nc.psum_tensor(f"ps_o{i}", [128, 1024], F32)) for i in range(2)]

        xr = [P5.enter_context(nc.sbuf_tensor(f"xr{i}", [128, 1024], F32, side="right")) for i in range(2)]
        add_v = [0, 0]
        st_c = [0, 0]
        gelu_free = [0, 0]
        stt_free = [0, 0]
        w2_loads = {}
        consume_done_t = 0
        steps = [(e_i, hf) for e_i in range(E) for hf in range(2)]
        for idx, (e_i, hf) in enumerate(steps):
            w2_loads[(e_i, hf)] = dma_batch("G", "gw2",
                                            [(w2buf[:, c * 1024:(c + 1) * 1024],
                                              w2_d[e_i, (hf * 16 + c) * 128:(hf * 16 + c + 1) * 128, :]) for c in range(16)],
                                            waits=[("tS", consume_done_t)])
            gel = 0
            if idx == 0:
                pl.op("S", None, waits=[("yc", pl.cnt["yc"])])
            for c in range(16):
                for th in range(2):
                    par = (c * 2 + th) % 2
                    tv = None
                    for sub in range(2):
                        for dc in range(8):
                            w0 = []
                            if sub == 0 and dc == 0:
                                w0 = [("sS", gelu_free[par]), ("gw1", w1_loads[(e_i, hf)])]
                            tv = pl.op("T", lambda e, w=w1buf[:, dc * 2048 + c * 128: dc * 2048 + (c + 1) * 128],
                                       x=xn_own[:, dc * 2048 + th * 1024 + sub * 512: dc * 2048 + th * 1024 + (sub + 1) * 512],
                                       o=ps_h[par][:, sub * 512:(sub + 1) * 512], st=(dc == 0), sp=(dc == 7):
                                       e.matmul(o, w, x, start=st, stop=sp),
                                       inc=("tS", 1) if (sub == 1 and dc == 7) else None, waits=w0)
                    gel = pl.op("S", lambda e, i=ps_h[par][:],
                                o=h_store[:, c * 2048 + th * 1024: c * 2048 + (th + 1) * 1024],
                                b=b1_sb[:, e_i * 32 + hf * 16 + c: e_i * 32 + hf * 16 + c + 1]:
                                e.activation(o, i, AF.Gelu, bias=b),
                                inc=("sS", 1), waits=[("tS", tv)])
                    gelu_free[par] = gel
            produce_done_t = pl.cnt["tS"]
            if idx + 1 < len(steps):
                e2, h2 = steps[idx + 1]
                w1_loads[(e2, h2)] = dma_batch("G", "gw1",
                                               [(w1buf[:, dc * 2048:(dc + 1) * 2048],
                                                 w1_d[e2, dc * 128:(dc + 1) * 128, h2 * 2048:(h2 + 1) * 2048]) for dc in range(8)],
                                               waits=[("tS", produce_done_t)])
            for tj in range(NT):
                rot = tj % 2
                ldp = None
                if idx == len(steps) - 1:
                    xsem, ssem = "yx%d" % rot, "ys%d" % rot
                    ldp = dma_batch("Y", xsem, [(xr[rot][:], xo_d[tj * 128:(tj + 1) * 128, :])],
                                    waits=[("vS", add_v[rot]), (ssem, st_c[rot])])
                tv = None
                for c in range(16):
                    for dh in range(2):
                        w0 = []
                        if c == 0 and dh == 0:
                            w0 = [("sS", gel), ("gw2", w2_loads[(e_i, hf)]), ("vS", stt_free[rot])]
                        tv = pl.op("T", lambda e, hh=h_store[:, c * 2048 + tj * 128: c * 2048 + (tj + 1) * 128],
                                   w=w2buf[:, c * 1024 + dh * 512: c * 1024 + (dh + 1) * 512],
                                   o=ps_o[rot][:, dh * 512:(dh + 1) * 512], st=(c == 0), sp=(c == 15):
                                   e.matmul(o, hh, w, start=st, stop=sp),
                                   inc=("tS", 1) if (c == 15 and dh == 1) else None, waits=w0)
                sv = pl.op("V", lambda e, p=ps_o[rot][:], a=acc[:, tj * 1024:(tj + 1) * 1024],
                           g=g_sb[:, tj * 8 + e_i: tj * 8 + e_i + 1]:
                           e.scalar_tensor_tensor(a, p, g, a, ALU.mult, ALU.add),
                           inc=("vS", 1), waits=[("tS", tv)])
                stt_free[rot] = sv
                if idx == len(steps) - 1:
                    xsem, ssem = "yx%d" % rot, "ys%d" % rot
                    av = pl.op("V", lambda e, x=xr[rot][:], a=acc[:, tj * 1024:(tj + 1) * 1024]:
                               e.tensor_add(x, x, a), inc=("vS", 1),
                               waits=[(xsem, ldp), ("vS", sv)])
                    add_v[rot] = av
                    st_c[rot] = dma_batch("Y", ssem, [(out_d[tj * 128:(tj + 1) * 128, :], xr[rot][:])],
                                          waits=[("vS", av)])
            consume_done_t = pl.cnt["tS"]
        dbg_dump("dbg_acc5", acc[:])
        pl.barrier()
    w1buf_st.close()

    for s in ("ys0", "ys1", "yc", "yx0", "yx1"):
        pl.op("Y", None, waits=[(s, pl.cnt[s])])
    acc_st.close()
    glb.close()

    with ExitStack() as R:
        sems = {s: R.enter_context(nc.semaphore(s)) for s in ("bar", "tS", "vS", "sS") + DMA_SEMS}
        blk = R.enter_context(nc.Block())
        pl.render(blk, sems)
    return nc



def _host_arrays(inputs):
    f32 = np.float32
    C = np.ascontiguousarray
    W_DQ = np.asarray(inputs["L0_attn_W_DQ"], f32)
    W_UQ = np.asarray(inputs["L0_attn_W_UQ"], f32)
    W_QR = np.asarray(inputs["L0_attn_W_QR"], f32)
    W_DKV = np.asarray(inputs["L0_attn_W_DKV"], f32)
    W_UK = np.asarray(inputs["L0_attn_W_UK"], f32)
    W_UV = np.asarray(inputs["L0_attn_W_UV"], f32)
    W_KR = np.asarray(inputs["L0_attn_W_KR"], f32)
    W_O = np.asarray(inputs["L0_attn_W_O"], f32)

    pq = []
    for h in range(H):
        pq += [h * 32 + 2 * f for f in range(16)]
    for h in range(H):
        pq += [h * 32 + 2 * f + 1 for f in range(16)]
    pq = np.array(pq)
    p32 = np.array([2 * f for f in range(16)] + [2 * f + 1 for f in range(16)])

    shared = {
        "wdq_t": C(W_DQ.T), "wdkv_t": C(W_DKV.T), "wkr_t": C(W_KR[p32, :].T),
        "wuq_t": C(W_UQ.T), "wqr_t": C(W_QR[pq, :].T), "wuk_t": C(W_UK.T),
        "wuv_t": C(W_UV.T), "wo_t": C(W_O.T),
        "wg": C(np.asarray(inputs["L0_ffn_Wg"], f32)),
        "bg2": C(np.asarray(inputs["L0_ffn_bg"], f32).reshape(E, 1)),
        "b2r": C(np.asarray(inputs["L0_ffn_b2"], f32)),
        "w1": C(np.asarray(inputs["L0_ffn_W1"], f32)),
        "w2": C(np.asarray(inputs["L0_ffn_W2"], f32)),
        "i128": np.eye(128, dtype=ml_dtypes.bfloat16),
        "ones1": np.ones((65, 128), dtype=ml_dtypes.bfloat16),
    }
    b1 = np.asarray(inputs["L0_ffn_b1"], f32)
    b1r = np.empty((128, E * 32), f32)
    for e in range(E):
        b1r[:, e * 32:(e + 1) * 32] = b1[e].reshape(32, 128).T
    shared["b1r"] = C(b1r)

    inv = 10000.0 ** (-np.arange(0, 32, 2, dtype=np.float64) / 32.0)
    ang = inv[:, None] * np.arange(S, dtype=np.float64)[None, :]
    shared["cosk"] = np.cos(ang).astype(ml_dtypes.bfloat16)
    shared["sink"] = np.sin(ang).astype(ml_dtypes.bfloat16)
    return shared, inv


def _per_core(inputs, core, inv):
    f32 = np.float32
    x = np.asarray(inputs["L0_attn_x"], f32)
    b, p = core // 2, core % 2
    tiles = [2 * j + p for j in range(NT)]
    xb = x[b]
    xo = np.ascontiguousarray(xb.reshape(32, 128, D)[tiles].reshape(NTOK, D))
    pos_own = (np.array(tiles)[:, None] * 128 + np.arange(128)[None, :]).reshape(-1).astype(np.float64)
    ang = inv[:, None] * pos_own[None, :]
    cosq = np.tile(np.cos(ang), (8, 1)).astype(ml_dtypes.bfloat16)
    sinq = np.tile(np.sin(ang), (8, 1)).astype(ml_dtypes.bfloat16)
    ONESM = np.ones((128, 128), f32)
    ZEROM = np.zeros((128, 128), f32)
    DIAG = np.triu(np.ones((128, 128), f32))

    def pat(m, off):
        t = off + p
        return ONESM if m < t else (DIAG if m == t else ZEROM)

    masks = np.concatenate([np.concatenate([pat(m, 0), pat(m, 2)], axis=1) for m in range(4)], axis=1)
    return {
        "xf": np.ascontiguousarray(xb),
        "xo": xo,
        "cosq": cosq,
        "sinq": sinq,
        "masks": masks.astype(ml_dtypes.bfloat16),
    }


_NC_CACHE = {}


def kernel(**inputs):
    shared, inv = _host_arrays(inputs)
    in_maps = []
    for c in range(8):
        m = dict(shared)
        m.update(_per_core(inputs, c, inv))
        in_maps.append(m)
    if "main" not in _NC_CACHE:
        _NC_CACHE["main"] = build_program(debug=False)
    nc = _NC_CACHE["main"]
    res = run_bass_kernel_spmd(nc, in_maps, core_ids=list(range(8)))
    out = np.empty((B, S, D), np.float32)
    for c in range(8):
        b, p = c // 2, c % 2
        tiles = [2 * j + p for j in range(NT)]
        out[b].reshape(32, 128, D)[tiles] = res.results[c]["out"].reshape(NT, 128, D)
    return out


# version 27
# speedup vs baseline: 1.2105x; 1.2105x over previous
import numpy as np
import ml_dtypes
import concourse.bass as bass
import concourse.mybir as mybir
from concourse.bass_utils import run_bass_kernel_spmd
from contextlib import ExitStack

F32 = mybir.dt.float32
BF16 = mybir.dt.bfloat16
AF = mybir.ActivationFunctionType
ALU = mybir.AluOpType
AX = mybir.AxisListType

B, S, D = 4, 4096, 1024
H, DH, DHR = 16, 64, 32
DLAT = 512
E, HID = 8, 4096
NTOK, NT, NST = 2048, 16, 8
RS96 = float(1.0 / np.sqrt(np.float32(96.0)))
CLIP_HI = float(np.exp(np.float32(80.0) * np.float32(RS96)))
CLIP_LO = float(np.exp(np.float32(-80.0) * np.float32(RS96)))
LN_EPS = 1e-5

ENGS = ("T", "V", "S", "Y", "G")


class Plan:

    def __init__(self):
        self.ops = {e: [] for e in ENGS}
        self.cnt = {}
        self.hwm = {e: {} for e in ENGS}

    def sem(self, name):
        self.cnt[name] = 0
        for e in ENGS:
            self.hwm[e][name] = 0

    def op(self, eng, fn, inc=None, waits=()):
        ws = []
        for (s, v) in waits:
            if v > self.hwm[eng][s]:
                ws.append((s, v))
                self.hwm[eng][s] = v
        if inc is not None:
            s, k = inc
            self.cnt[s] += k
        if fn is not None or inc is not None or ws:
            self.ops[eng].append((fn, inc, ws))
        return self.cnt[inc[0]] if inc is not None else None

    def snap(self):
        return dict(self.cnt)

    def barrier(self):
        for s in ("yc", "yx0", "yx1", "ys0", "ys1"):
            self.op("Y", None, waits=[(s, self.cnt[s])])
        for s in ("ga", "gq", "gw1", "gw2"):
            self.op("G", None, waits=[(s, self.cnt[s])])
        for e in ENGS:
            self.op(e, lambda eng: eng.nop(), inc=("bar", 1))
        tgt = self.cnt["bar"]
        for e in ENGS:
            self.op(e, None, waits=[("bar", tgt)])

    def render(self, blk, sems):
        def mk(key):
            ops = self.ops[key]

            def body(eng):
                for (fn, inc, ws) in ops:
                    for (s, v) in ws:
                        eng.wait_ge(sems[s], v)
                    if fn is None:
                        if inc is not None:
                            eng.nop().then_inc(sems[inc[0]], inc[1])
                    else:
                        inst = fn(eng)
                        if inc is not None:
                            inst.then_inc(sems[inc[0]], inc[1])
            return body

        blk.tensor(mk("T"))
        blk.vector(mk("V"))
        blk.scalar(mk("S"))
        blk.sync(mk("Y"))
        blk.gpsimd(mk("G"))


def build_program(debug=False):
    nc = bass.Bass("TRN2", target_bir_lowering=False, debug=False)
    pl = Plan()
    DMA_SEMS = ("yc", "yx0", "yx1", "ys0", "ys1", "ga", "gq", "gw1", "gw2")
    for s in ("bar", "tS", "vS", "sS") + DMA_SEMS:
        pl.sem(s)

    d = {}

    def din(name, shape, dt=F32):
        d[name] = nc.dram_tensor(name, list(shape), dt, kind="ExternalInput").ap()
        return d[name]

    xf_d = din("xf", (S, D))
    xo_d = din("xo", (NTOK, D))
    cosk_d = din("cosk", (16, S), BF16)
    sink_d = din("sink", (16, S), BF16)
    cosq_d = din("cosq", (128, NTOK), BF16)
    sinq_d = din("sinq", (128, NTOK), BF16)
    masks_d = din("masks", (128, 4 * 256), BF16)
    i128_d = din("i128", (128, 128), BF16)
    ones_d = din("ones1", (65, 128), BF16)
    wdq_d = din("wdq_t", (D, DLAT))
    wdkv_d = din("wdkv_t", (D, DLAT))
    wkr_d = din("wkr_t", (D, DHR))
    wuq_d = din("wuq_t", (DLAT, H * DH))
    wqr_d = din("wqr_t", (DLAT, H * DHR))
    wuk_d = din("wuk_t", (DLAT, H * DH))
    wuv_d = din("wuv_t", (DLAT, H * DH))
    wo_d = din("wo_t", (D, D))
    wg_d = din("wg", (D, E))
    bg_d = din("bg2", (E, 1))
    b1_d = din("b1r", (128, E * 32))
    b2_d = din("b2r", (E, D))
    w1_d = din("w1", (E, D, HID))
    w2_d = din("w2", (E, HID, D))

    out_d = nc.dram_tensor("out", [NTOK, D], F32, kind="ExternalOutput").ap()
    dbg = {}
    if debug:
        for nm, shape, dt in [
            ("dbg_xn", (128, 8 * 2048), BF16), ("dbg_ckv", (128, 4 * 4096), BF16),
            ("dbg_kr", (48, 4096), BF16), ("dbg_cq", (128, 4 * 2048), BF16),
            ("dbg_g", (128, 128), F32), ("dbg_lgt", (128, 128), F32),
            ("dbg_attn", (128, 8 * 2048), BF16), ("dbg_acc4", (128, 16 * 1024), BF16),
            ("dbg_acc5", (128, 16 * 1024), BF16), ("dbg_qh", (96, 2048), BF16),
            ("dbg_k", (96, 4096), BF16), ("dbg_v", (128, 32 * 260), BF16),
        ]:
            dbg[nm] = nc.dram_tensor(nm, list(shape), dt, kind="ExternalOutput").ap()

    def dma_batch(eng, sem, pairs, waits=()):
        c = None
        first = [(sem, pl.cnt[sem])] + list(waits)
        for i, (dst, srcap) in enumerate(pairs):
            c = pl.op(eng, lambda e, o=dst, s=srcap: e.dma_start(out=o, in_=s),
                      inc=(sem, 16), waits=first if i == 0 else ())
        return c

    def dbg_dump(nm, ap, dst=None):
        if debug:
            pl.op("Y", None, waits=[("vS", pl.cnt["vS"]), ("sS", pl.cnt["sS"]), ("tS", pl.cnt["tS"])])
            dma_batch("Y", "yc", [(dbg[nm] if dst is None else dst, ap)])
            pl.barrier()

    glb = ExitStack()
    sb = lambda name, shape, dt: glb.enter_context(nc.sbuf_tensor(name, list(shape), dt))

    xn_own = sb("xn_own", (128, 8 * 2048), BF16)
    g_sb = sb("g_sb", (128, 16 * 8), F32)
    g_T = sb("g_T", (8, 2048), BF16)
    b2_sb = sb("b2_sb", (8, 1024), BF16)
    b1_sb = sb("b1_sb", (128, E * 32), F32)
    ones1 = sb("ones1s", (65, 128), BF16)
    eps_col = sb("eps_col", (128, 1), F32)
    kr_st = ExitStack()
    kr = kr_st.enter_context(nc.sbuf_tensor("kr", [48, 4096], BF16))
    kv_st = ExitStack()
    ckv = kv_st.enter_context(nc.sbuf_tensor("ckv", [128, 4 * 4096], BF16, side="right"))
    cq = kv_st.enter_context(nc.sbuf_tensor("cq", [128, 4 * 2048], BF16, side="right"))
    cc_st = ExitStack()

    pl.op("V", lambda e: e.memset(eps_col[:], LN_EPS), inc=("vS", 1))
    dma_batch("Y", "yc", [(ones1[:], ones_d), (b1_sb[:], b1_d)])
    dma_batch("G", "ga", [(b2_sb[:], b2_d)])

    with ExitStack() as P1:
        sb1 = lambda name, shape, dt: P1.enter_context(nc.sbuf_tensor(name, list(shape), dt, side="right"))
        ps1 = lambda name, shape, dt=F32: P1.enter_context(nc.psum_tensor(name, list(shape), dt))
        x_st = [sb1(f"x_st{i}", (128, 4096), F32) for i in range(2)]
        xn_tm = [sb1(f"xn_tm{i}", (128, 4096), BF16) for i in range(2)]
        xn_fm = [sb1(f"xn_fm{i}", (128, 8 * 512), BF16) for i in range(2)]
        cosk = sb1("cosk_s", (16, S), BF16)
        sink = sb1("sink_s", (16, S), BF16)
        wdkv = sb1("wdkv_s", (128, 8 * 512), BF16)
        wkr = sb1("wkr_s", (128, 8 * 32), BF16)
        wdq = sb1("wdq_s", (128, 8 * 512), BF16)
        wg = sb1("wg_s", (128, 8 * 8), BF16)
        lgbf = sb1("lgbf", (8, 2048), BF16)
        sq_s = sb1("sq_s", (128, 1024), BF16)
        mcol = [sb1(f"mcol{i}", (128, 16), F32) for i in range(2)]
        t1_s = sb1("t1_s", (16, 512), BF16)
        t2_s = sb1("t2_s", (16, 512), BF16)
        i128 = sb1("i128s", (128, 128), BF16)
        bg_sb = sb1("bg_sb", (8, 1), F32)
        lgt_sb = sb1("lgt_sb", (128, 16 * 8), F32)
        g_bf = sb1("g_bf", (128, 16 * 8), BF16)
        c_i128 = None
        ps_tr = [ps1(f"ps_tr{i}", (128, 512), BF16) for i in range(2)]
        ps_pr = [ps1(f"ps_pr{i}", (128, 512)) for i in range(4)]
        ps_kra = ps1("ps_kra", (16, 512))
        ps_krb = ps1("ps_krb", (16, 512))

        c_yc1 = dma_batch("Y", "yc", [(i128[:], i128_d), (bg_sb[:], bg_d),
                                      (cosk[:], cosk_d), (sink[:], sink_d)])
        c_ga1 = dma_batch("G", "ga",
                          [(wdkv[:, dc * 512:(dc + 1) * 512], wdkv_d[dc * 128:(dc + 1) * 128, :]) for dc in range(8)]
                          + [(wkr[:, dc * 32:(dc + 1) * 32], wkr_d[dc * 128:(dc + 1) * 128, :]) for dc in range(8)]
                          + [(wdq[:, dc * 512:(dc + 1) * 512], wdq_d[dc * 128:(dc + 1) * 128, :]) for dc in range(8)]
                          + [(wg[:, dc * 8:(dc + 1) * 8], wg_d[dc * 128:(dc + 1) * 128, :]) for dc in range(8)])
        c_wdkv = c_wkr = c_wdq = c_wg = c_ga1

        stage_free = [None, None]
        tr_evacs = []

        def emit_means(par, row0, xdram):
            w0 = []
            if stage_free[par] is not None:
                sf = stage_free[par]
                w0 = [("tS", sf["tS"]), ("vS", sf["vS"]), ("sS", sf["sS"])]
            xsem = "yx%d" % par
            ldc = dma_batch("Y", xsem,
                            [(x_st[par][:, k * 1024:(k + 1) * 1024],
                              xdram[row0 + k * 128: row0 + (k + 1) * 128, :]) for k in range(4)],
                            waits=w0)
            mc = mcol[par]
            cols = [(x_st[par][:, k * 1024:(k + 1) * 1024],
                     mc[:, k * 4 + 0: k * 4 + 1], mc[:, k * 4 + 1: k * 4 + 2],
                     mc[:, k * 4 + 2: k * 4 + 3], mc[:, k * 4 + 3: k * 4 + 4]) for k in range(4)]
            mv = []
            for k in range(4):
                xs, c_s, c_nm, c_sd, c_rs = cols[k]
                mv.append(pl.op("V", lambda e, i=xs, o=c_s: e.reduce_sum(o, i, axis=AX.X),
                                inc=("vS", 1), waits=[(xsem, ldc)]))
            return cols, mv

        def ln_round(par, pre, dst_fn, consumer):
            cols, mv = pre
            sqv = [None] * 4
            for k in range(4):
                xs, c_s, c_nm, c_sd, c_rs = cols[k]
                sv = pl.op("S", lambda e, i=c_s, o=c_nm: e.mul(o, i, -1.0 / 1024.0),
                           inc=("sS", 1), waits=[("vS", mv[k])])
                sqv[k] = pl.op("S", lambda e, i=xs, o=sq_s[:], b=c_nm, a=c_sd:
                               e.activation(o, i, AF.Square, bias=b, accum_out=a),
                               inc=("sS", 1), waits=[("sS", sv)])
            sqr = []
            for k in range(4):
                xs, c_s, c_nm, c_sd, c_rs = cols[k]
                sqr.append(pl.op("S", lambda e, i=c_sd, o=c_sd:
                                 e.activation(o, i, AF.Sqrt, scale=1.0 / 1024.0, bias=eps_col[:]),
                                 inc=("sS", 1), waits=[("sS", sqv[k])]))
            rcv = []
            for k in range(4):
                xs, c_s, c_nm, c_sd, c_rs = cols[k]
                rcv.append(pl.op("V", lambda e, i=c_sd, o=c_rs: e.reciprocal(o, i),
                                 inc=("vS", 1), waits=[("sS", sqr[k])]))
            for k in range(4):
                xs, c_s, c_nm, c_sd, c_rs = cols[k]
                vv = pl.op("V", lambda e, i=xs, o=xn_tm[par][:, k * 1024:(k + 1) * 1024], m=c_nm, r=c_rs:
                           e.tensor_scalar(o, i, m, r, ALU.add, ALU.mult), inc=("vS", 1),
                           waits=[("vS", rcv[k])])
                evh = []
                for h in range(2):
                    rot = len(tr_evacs) % 2
                    gate = tr_evacs[-2] if len(tr_evacs) >= 2 else ("sS", 0)
                    for j in range(4):
                        dc = h * 4 + j
                        w0 = []
                        if j == 0:
                            w0 = [gate]
                            if h == 0:
                                w0 += [("vS", vv), ("yc", c_yc1)]
                        tv = pl.op("T", lambda e, i=xn_tm[par][:, k * 1024 + dc * 128: k * 1024 + (dc + 1) * 128],
                                   o=ps_tr[rot][:, j * 128:(j + 1) * 128]: e.transpose(o, i, i128[:]),
                                   inc=("tS", 1) if j == 3 else None,
                                   waits=w0)
                    ev = pl.op("S", lambda e, i=ps_tr[rot][:].rearrange("p (c w) -> p c w", c=4),
                               o=dst_fn(k, h): e.copy(o, i),
                               inc=("sS", 1), waits=[("tS", tv)])
                    evh.append(ev)
                    tr_evacs.append(("sS", ev))
                consumer(k, tuple(evh))
            return tr_evacs[-1][1]

        pb_free = [0, 0, 0, 0]
        kr_free = [0]
        lg_free = [0]

        pre = emit_means(0, 0, xf_d)
        for r in range(8):
            par = r % 2
            tvk = {}

            def consumer(k, ev, par=par, r=r, tvk=tvk):
                evA, evB = ev
                for half in range(2):
                    pso = ps_kra if half == 0 else ps_krb
                    for dc in range(8):
                        w0 = []
                        if dc == 4 and half == 0:
                            w0 = [("sS", evB)]
                        if dc == 0 and half == 0:
                            w0 = [("sS", evA)]
                            if k == 0:
                                w0 += [("ga", c_wkr), ("vS", kr_free[0])]
                        tv = pl.op("T", lambda e, w=wkr[:, dc * 32 + half * 16: dc * 32 + half * 16 + 16],
                                   x=xn_fm[par][:, dc * 512 + k * 128: dc * 512 + (k + 1) * 128],
                                   o=pso[:, k * 128:(k + 1) * 128],
                                   st=(k == 0 and dc == 0), sp=(k == 3 and dc == 7):
                                   e.matmul(o, w, x, start=st, stop=sp),
                                   inc=("tS", 1) if (k == 3 and dc == 7 and half == 1) else None,
                                   waits=w0)
                    if k == 3 and half == 1:
                        tvk["tvb"] = tv
                for lc in range(4):
                    tv = None
                    for dc in range(8):
                        w0 = []
                        if dc == 0:
                            w0 = [("sS", evA)]
                            if k == 0:
                                w0 += [("ga", c_wdkv), ("sS", pb_free[lc])]
                        if dc == 4 and lc == 0:
                            w0 = [("sS", evB)]
                        tv = pl.op("T", lambda e, w=wdkv[:, dc * 512 + lc * 128: dc * 512 + (lc + 1) * 128],
                                   x=xn_fm[par][:, dc * 512 + k * 128: dc * 512 + (k + 1) * 128],
                                   o=ps_pr[lc][:, k * 128:(k + 1) * 128],
                                   st=(k == 0 and dc == 0), sp=(k == 3 and dc == 7):
                                   e.matmul(o, w, x, start=st, stop=sp),
                                   inc=("tS", 1) if (k == 3 and dc == 7) else None,
                                   waits=w0)
                    if k == 3:
                        ev2 = pl.op("S", lambda e, i=ps_pr[lc][:], o=ckv[:, lc * 4096 + r * 512: lc * 4096 + (r + 1) * 512]:
                                    e.copy(o, i), inc=("sS", 1), waits=[("tS", tv)])
                        pb_free[lc] = ev2

            ln_round(par, pre,
                     lambda k, h, par=par: xn_fm[par][:].rearrange("p (c w) -> p c w", c=8)[:, h * 4:h * 4 + 4, k * 128:(k + 1) * 128],
                     consumer)
            tvb = tvk["tvb"]
            if r < 7:
                pre = emit_means((r + 1) % 2, (r + 1) * 512, xf_d)
            else:
                pre = emit_means(0, 0, xo_d)
            ck = cosk[:, r * 512:(r + 1) * 512]
            sk = sink[:, r * 512:(r + 1) * 512]
            t1h = t1_s[0:16, 0:512]
            t2h = t2_s[0:16, 0:512]
            rc = pl.op("V", lambda e, c=ck: e.tensor_mul(t1h, ps_kra[:], c),
                       inc=("vS", 1), waits=[("tS", tvb), ("yc", c_yc1)])
            rc = pl.op("V", lambda e, s=sk: e.tensor_mul(t2h, ps_krb[:], s), inc=("vS", 1))
            rc = pl.op("V", lambda e, o=kr[0:16, r * 512:(r + 1) * 512]: e.tensor_sub(o, t1h, t2h),
                       inc=("vS", 1), waits=[("vS", rc)])
            rc = pl.op("V", lambda e, s=sk: e.tensor_mul(t1h, ps_kra[:], s),
                       inc=("vS", 1), waits=[("vS", rc)])
            rc = pl.op("V", lambda e, c=ck: e.tensor_mul(t2h, ps_krb[:], c), inc=("vS", 1))
            kf = pl.op("V", lambda e, o=kr[32:48, r * 512:(r + 1) * 512]: e.tensor_add(o, t1h, t2h),
                       inc=("vS", 1), waits=[("vS", rc)])
            kr_free[0] = kf
            stage_free[par] = pl.snap()

        for r in range(4):
            par = r % 2

            def consumer(k, ev, par=par, r=r):
                evA, evB = ev
                for lc in range(4):
                    tv = None
                    for dc in range(8):
                        w0 = []
                        if dc == 0:
                            w0 = [("sS", evA)]
                            if k == 0:
                                w0 += [("ga", c_wdq), ("sS", pb_free[lc])]
                        if dc == 4 and lc == 0:
                            w0 = [("sS", evB)]
                        tv = pl.op("T", lambda e, w=wdq[:, dc * 512 + lc * 128: dc * 512 + (lc + 1) * 128],
                                   x=xn_own[:, dc * 2048 + r * 512 + k * 128: dc * 2048 + r * 512 + (k + 1) * 128],
                                   o=ps_pr[lc][:, k * 128:(k + 1) * 128],
                                   st=(k == 0 and dc == 0), sp=(k == 3 and dc == 7):
                                   e.matmul(o, w, x, start=st, stop=sp),
                                   inc=("tS", 1) if (k == 3 and dc == 7) else None,
                                   waits=w0)
                    if k == 3:
                        ev2 = pl.op("S", lambda e, i=ps_pr[lc][:], o=cq[:, lc * 2048 + r * 512: lc * 2048 + (r + 1) * 512]:
                                    e.copy(o, i), inc=("sS", 1), waits=[("tS", tv)])
                        pb_free[lc] = ev2
                tv = None
                for dc in range(8):
                    w0 = []
                    if dc == 0:
                        w0 = [("sS", evA)]
                        if k == 0:
                            w0 += [("ga", c_wg), ("sS", lg_free[0]), ("vS", kr_free[0])]
                    tv = pl.op("T", lambda e, w=wg[:, dc * 8:(dc + 1) * 8],
                               x=xn_own[:, dc * 2048 + r * 512 + k * 128: dc * 2048 + r * 512 + (k + 1) * 128],
                               o=ps_kra[0:8, k * 128:(k + 1) * 128],
                               st=(k == 0 and dc == 0), sp=(k == 3 and dc == 7):
                               e.matmul(o, w, x, start=st, stop=sp),
                               inc=("tS", 1) if (k == 3 and dc == 7) else None,
                               waits=w0)
                if k == 3:
                    sv = pl.op("S", lambda e, i=ps_kra[0:8, :], o=lgbf[0:8, r * 512:(r + 1) * 512]:
                               e.activation(o, i, AF.Identity, bias=bg_sb[0:8, 0:1]),
                               inc=("sS", 1), waits=[("tS", tv), ("yc", c_yc1)])
                    lg_free[0] = sv

            ln_round(par, pre,
                     lambda k, h, r=r: xn_own[:].rearrange("p (c w) -> p c w", c=8)[:, h * 4:h * 4 + 4, r * 512 + k * 128: r * 512 + (k + 1) * 128],
                     consumer)
            if r < 3:
                pre = emit_means((r + 1) % 2, (r + 1) * 512, xo_d)
            stage_free[par] = pl.snap()

        lgt_ev = []
        for tj in range(NT):
            pst = ps_pr[2 + tj % 2][0:128, 0:8]
            gate = lgt_ev[-2] if len(lgt_ev) >= 2 else 0
            w0 = [("sS", lg_free[0]), ("vS", gate)]
            if len(lgt_ev) < 2:
                w0.append(("sS", pl.cnt["sS"]))
            tv = pl.op("T", lambda e, i=lgbf[0:8, tj * 128:(tj + 1) * 128], o=pst:
                       e.matmul(o, i, i128[0:8, 0:8], start=True, stop=True),
                       inc=("tS", 1), waits=w0)
            ev = pl.op("V", lambda e, o=lgt_sb[:, tj * 8:(tj + 1) * 8], i=pst: e.tensor_copy(o, i),
                       inc=("vS", 1), waits=[("tS", tv)])
            lgt_ev.append(ev)

        gsc = sb1("gsc", (128, 16 * 8), F32)
        m1s = sb1("m1s", (128, 16), F32)
        m2s = sb1("m2s", (128, 16), F32)
        dens = sb1("dens", (128, 16), F32)
        negm = sb1("negm", (128, 16), F32)
        e_s = sb1("e_s", (128, 16 * 8), F32)
        lgt3 = lgt_sb[:].rearrange("p (t e) -> p t e", t=16)
        gsc3 = gsc[:].rearrange("p (t e) -> p t e", t=16)
        vv = pl.op("V", lambda e: e.reduce_max(m1s[:].rearrange("p (t o) -> p t o", o=1), lgt3, axis=AX.X),
                   inc=("vS", 1), waits=[("vS", lgt_ev[-1])])
        sv = pl.op("S", lambda e: e.mul(negm[:], m1s[:], -1.0), inc=("sS", 1), waits=[("vS", vv)])
        for tj in range(NT):
            sv = pl.op("S", lambda e, i=lgt_sb[:, tj * 8:(tj + 1) * 8], o=e_s[:, tj * 8:(tj + 1) * 8],
                       b=negm[:, tj:tj + 1]: e.activation(o, i, AF.Exp, bias=b), inc=("sS", 1),
                       waits=[("sS", sv)])
        for tj in range(NT):
            c1 = pl.op("V", lambda e, i=lgt_sb[:, tj * 8:(tj + 1) * 8], o=gsc[:, tj * 8:(tj + 1) * 8],
                       m=m1s[:, tj:tj + 1]: e.tensor_scalar(o, i, m, None, ALU.is_ge),
                       inc=("vS", 1), waits=[("sS", sv)])
            pl.op("V", lambda e, i=gsc[:, tj * 8:(tj + 1) * 8], l=lgt_sb[:, tj * 8:(tj + 1) * 8],
                  o=gsc[:, tj * 8:(tj + 1) * 8]: e.scalar_tensor_tensor(o, i, -1e30, l, ALU.mult, ALU.add),
                  inc=("vS", 1), waits=[("vS", c1)])
        vv = pl.op("V", lambda e: e.reduce_max(m2s[:].rearrange("p (t o) -> p t o", o=1), gsc3, axis=AX.X),
                   inc=("vS", 1), waits=[("vS", pl.cnt["vS"])])
        sv = pl.op("S", lambda e: e.mul(m2s[:], m2s[:], 1.0), inc=("sS", 1), waits=[("vS", vv)])
        for tj in range(NT):
            c1 = pl.op("V", lambda e, i=lgt_sb[:, tj * 8:(tj + 1) * 8], o=gsc[:, tj * 8:(tj + 1) * 8],
                       m=m2s[:, tj:tj + 1]: e.tensor_scalar(o, i, m, None, ALU.is_ge),
                       inc=("vS", 1), waits=[("sS", sv)])
            pl.op("V", lambda e, a=e_s[:, tj * 8:(tj + 1) * 8], o=gsc[:, tj * 8:(tj + 1) * 8],
                  i=gsc[:, tj * 8:(tj + 1) * 8]:
                  e.tensor_mul(o, i, a),
                  inc=("vS", 1), waits=[("vS", c1)])
        vv = pl.op("V", lambda e: e.reduce_sum(dens[:].rearrange("p (t o) -> p t o", o=1), gsc3, axis=AX.X),
                   inc=("vS", 1), waits=[("vS", pl.cnt["vS"])])
        sv = pl.op("S", lambda e: e.mul(dens[:], dens[:], 1.0), inc=("sS", 1), waits=[("vS", vv)])
        vv = pl.op("V", lambda e: e.reciprocal(m1s[:], dens[:]), inc=("vS", 1), waits=[("sS", sv)])
        for tj in range(NT):
            vv = pl.op("V", lambda e, i=gsc[:, tj * 8:(tj + 1) * 8], o=g_sb[:, tj * 8:(tj + 1) * 8],
                       r=m1s[:, tj:tj + 1]: e.tensor_scalar(o, i, r, None, ALU.mult),
                       inc=("vS", 1), waits=[("vS", vv)])
        vv = pl.op("V", lambda e: e.tensor_copy(g_bf[:], g_sb[:]), inc=("vS", 1), waits=[("vS", vv)])
        gT_ev = []
        for tj in range(NT):
            pst = ps_pr[2 + tj % 2][0:8, 0:128]
            gate = gT_ev[-2] if len(gT_ev) >= 2 else 0
            tv = pl.op("T", lambda e, i=g_bf[:, tj * 8:(tj + 1) * 8], o=pst:
                       e.matmul(o, i, i128[:], start=True, stop=True),
                       inc=("tS", 1), waits=[("vS", vv), ("vS", gate)])
            ev = pl.op("V", lambda e, o=g_T[0:8, tj * 128:(tj + 1) * 128], i=pst: e.tensor_copy(o, i),
                       inc=("vS", 1), waits=[("tS", tv)])
            gT_ev.append(ev)

        dbg_dump("dbg_xn", xn_own[:])
        dbg_dump("dbg_ckv", ckv[:])
        dbg_dump("dbg_kr", kr[0:16, :], dst=dbg["dbg_kr"][0:16, :] if debug else None)
        dbg_dump("dbg_kr", kr[32:48, :], dst=dbg["dbg_kr"][32:48, :] if debug else None)
        dbg_dump("dbg_cq", cq[:])
        dbg_dump("dbg_g", g_sb[:])
        dbg_dump("dbg_lgt", lgt_sb[:])
        pl.barrier()

    with ExitStack() as P3:
        sb3 = lambda name, shape, dt: P3.enter_context(nc.sbuf_tensor(name, list(shape), dt, side="right"))
        ps3 = lambda name, shape: P3.enter_context(nc.psum_tensor(name, list(shape), F32))
        Kb = [sb3(f"Kb{i}", (96, 4096), BF16) for i in range(2)]
        qh = [sb3(f"qh{i}", (96, 2048), BF16) for i in range(2)]
        kpref = dma_batch("G", "gq", [(Kb[0][64:80, :], kr[0:16, :]), (Kb[0][80:96, :], kr[32:48, :]),
                                      (Kb[1][64:80, :], kr[0:16, :]), (Kb[1][80:96, :], kr[32:48, :])])
        kr_st.close()
        attn_cc = cc_st.enter_context(nc.sbuf_tensor("attn_cc", [128, 8 * 2048], BF16))
        wuq_q = [sb3(f"wuq_q{i}", (128, 4 * 256), BF16) for i in range(2)]
        wuk_q = [sb3(f"wuk_q{i}", (128, 4 * 256), BF16) for i in range(2)]
        wqr_q = [sb3(f"wqr_q{i}", (128, 4 * 128), BF16) for i in range(2)]
        wuv_q = sb3("wuv_q", (128, 4 * 256), BF16)
        cosq = sb3("cosq_s", (128, 2048), BF16)
        sinq = sb3("sinq_s", (128, 2048), BF16)
        Vq = sb3("Vq", (128, 32 * 260), BF16)
        qra = sb3("qra", (64, 2048), BF16)
        qrb = sb3("qrb", (64, 2048), BF16)
        avbf = sb3("avbf", (64, 512), BF16)
        t1q = sb3("t1q", (64, 512), F32)
        t2q = sb3("t2q", (65, 512), F32)
        rcp_sb = t1q
        rrbf = t2q[64:65, :].bitcast(BF16)[:, 0:512]
        ptA = [sb3(f"ptA{i}", (128, 1024), BF16) for i in range(3)]
        masks = sb3("masks_s", (128, 4 * 256), BF16)
        ps_sc = [ps3(f"ps_sc{i}", (128, 1024)) for i in range(2)]
        ps_out = [ps3(f"ps_out{i}", (65, 512)) for i in range(2)]
        ps_b = [ps3("ps_b1", (64, 512)), ps3("ps_b2", (64, 512))]

        ctab = dma_batch("Y", "yc", [(cosq[:], cosq_d), (sinq[:], sinq_d), (masks[:], masks_d)])

        st_state = {
            "last_qk_t": 0, "last_pv_t": 0, "last_fin_v": 0, "last_avc": [("sS", 0), ("sS", 0)], "grp": 0,
            "last_exp_s": [0, 0], "last_pv_for_pt": [0, 0, 0],
            "pbv_free": [("vS", 0), ("vS", 0)], "dq_prev": 0,
            "qk_last": {},
            "bready": {},
            "wq_loads": {},
        }

        def load_quarter_weights(hq):
            par = hq % 2
            prs = [(wuq_q[par][:, lc * 256:(lc + 1) * 256], wuq_d[lc * 128:(lc + 1) * 128, hq * 256:(hq + 1) * 256]) for lc in range(4)]
            prs += [(wuk_q[par][:, lc * 256:(lc + 1) * 256], wuk_d[lc * 128:(lc + 1) * 128, hq * 256:(hq + 1) * 256]) for lc in range(4)]
            prs += [(wqr_q[par][:, lc * 128:lc * 128 + 64], wqr_d[lc * 128:(lc + 1) * 128, hq * 64:(hq + 1) * 64]) for lc in range(4)]
            prs += [(wqr_q[par][:, lc * 128 + 64:lc * 128 + 128], wqr_d[lc * 128:(lc + 1) * 128, 256 + hq * 64:256 + (hq + 1) * 64]) for lc in range(4)]
            st_state["wq_loads"][hq] = dma_batch("G", "ga", prs,
                                                 waits=[("tS", st_state["qk_last"].get(hq * 4 - 3, 0))])

        def build_quanta(h):
            hq2, hl2 = h // 4, h % 4
            par = h % 2
            wpar = hq2 % 2
            quanta = []

            def q_fc(fc):
                def go():
                    rot = fc % 2
                    pb = ps_b[rot]
                    tv = None
                    for lc in range(4):
                        tv = pl.op("T", lambda e, w=wuq_q[wpar][:, lc * 256 + hl2 * 64: lc * 256 + (hl2 + 1) * 64],
                                   r=cq[:, lc * 2048 + fc * 512: lc * 2048 + (fc + 1) * 512],
                                   o=pb[:], st=(lc == 0), sp=(lc == 3): e.matmul(o, w, r, start=st, stop=sp),
                                   inc=("tS", 1) if lc == 3 else None,
                                   waits=([("ga", st_state["wq_loads"][hq2]), st_state["pbv_free"][rot]] if lc == 0 else []))
                    eng = "V" if fc % 2 == 0 else "S"
                    sem = "vS" if eng == "V" else "sS"
                    if eng == "V":
                        ev = pl.op("V", lambda e, i=pb[:], o=qh[par][0:64, fc * 512:(fc + 1) * 512]: e.tensor_copy(o, i),
                                   inc=("vS", 1), waits=[("tS", tv), ("tS", st_state["qk_last"].get(h - 2, 0))])
                    else:
                        ev = pl.op("S", lambda e, i=pb[:], o=qh[par][0:64, fc * 512:(fc + 1) * 512]: e.copy(o, i),
                                   inc=("sS", 1), waits=[("tS", tv), ("tS", st_state["qk_last"].get(h - 2, 0))])
                    st_state["pbv_free"][rot] = (sem, ev)
                    st_state["bready"].setdefault(h, {})["q"] = (sem, ev)
                return go

            def k_fc(fc):
                def go():
                    rot = fc % 2
                    pb = ps_b[rot]
                    tv = None
                    for lc in range(4):
                        tv = pl.op("T", lambda e, w=wuk_q[wpar][:, lc * 256 + hl2 * 64: lc * 256 + (hl2 + 1) * 64],
                                   r=ckv[:, lc * 4096 + fc * 512: lc * 4096 + (fc + 1) * 512],
                                   o=pb[:], st=(lc == 0), sp=(lc == 3): e.matmul(o, w, r, start=st, stop=sp),
                                   inc=("tS", 1) if lc == 3 else None,
                                   waits=([("ga", st_state["wq_loads"][hq2]), st_state["pbv_free"][rot]] if lc == 0 else []))
                    eng = "V" if fc % 2 == 0 else "S"
                    sem = "vS" if eng == "V" else "sS"
                    if eng == "V":
                        ev = pl.op("V", lambda e, i=pb[:], o=Kb[par][0:64, fc * 512:(fc + 1) * 512]: e.tensor_copy(o, i),
                                   inc=("vS", 1), waits=[("tS", tv), ("tS", st_state["qk_last"].get(h - 2, 0)), ("gq", kpref)])
                    else:
                        ev = pl.op("S", lambda e, i=pb[:], o=Kb[par][0:64, fc * 512:(fc + 1) * 512]: e.copy(o, i),
                                   inc=("sS", 1), waits=[("tS", tv), ("tS", st_state["qk_last"].get(h - 2, 0)), ("gq", kpref)])
                    st_state["pbv_free"][rot] = (sem, ev)
                    st_state["bready"].setdefault(h, {})["k"] = (sem, ev)
                return go

            def dist():
                def go():
                    dq = dma_batch("G", "gq", [(qh[par][64:80, :], qra[hl2 * 16:(hl2 + 1) * 16, :]),
                                               (qh[par][80:96, :], qrb[hl2 * 16:(hl2 + 1) * 16, :])],
                                   waits=[("vS", st_state["qr_ev"]), ("tS", st_state["qk_last"].get(h - 2, 0))])
                    st_state["bready"].setdefault(h, {})["dq"] = dq
                    st_state["dq_prev"] = dq
                return go

            for fc in range(4):
                quanta.append(q_fc(fc))
            for fc in range(8):
                quanta.append(k_fc(fc))
            if hl2 != 0:
                quanta.append(dist())
            return quanta

        def quarter_start(hq, filler=()):
            filler = list(filler)
            par = hq % 2
            qr_ev = 0
            for fc in range(4):
                tvb = None
                for lc in range(4):
                    pl.op("T", lambda e, w=wqr_q[par][:, lc * 128:lc * 128 + 64],
                          r=cq[:, lc * 2048 + fc * 512: lc * 2048 + (fc + 1) * 512],
                          o=ps_b[0][:], st=(lc == 0), sp=(lc == 3): e.matmul(o, w, r, start=st, stop=sp),
                          waits=([("ga", st_state["wq_loads"][hq])] + list(st_state["pbv_free"]) if lc == 0 else []))
                for lc in range(4):
                    tvb = pl.op("T", lambda e, w=wqr_q[par][:, lc * 128 + 64:lc * 128 + 128],
                                r=cq[:, lc * 2048 + fc * 512: lc * 2048 + (fc + 1) * 512],
                                o=ps_b[1][:], st=(lc == 0), sp=(lc == 3): e.matmul(o, w, r, start=st, stop=sp),
                                inc=("tS", 1) if lc == 3 else None)
                cqs = cosq[0:64, fc * 512:(fc + 1) * 512]
                sqs = sinq[0:64, fc * 512:(fc + 1) * 512]
                rc = pl.op("V", lambda e, c=cqs: e.tensor_mul(t1q[:], ps_b[0][:], c),
                           inc=("vS", 1),
                           waits=[("tS", tvb), ("yc", ctab), ("gq", st_state["dq_prev"]),
                                  ("vS", st_state["last_fin_v"])])
                rc = pl.op("V", lambda e, s=sqs: e.tensor_mul(t2q[0:64, :], ps_b[1][:], s), inc=("vS", 1))
                rc = pl.op("V", lambda e, o=qra[:, fc * 512:(fc + 1) * 512]: e.tensor_sub(o, t1q[:], t2q[0:64, :]),
                           inc=("vS", 1), waits=[("vS", rc)])
                rc = pl.op("V", lambda e, s=sqs: e.tensor_mul(t1q[:], ps_b[0][:], s),
                           inc=("vS", 1), waits=[("vS", rc)])
                rc = pl.op("V", lambda e, c=cqs: e.tensor_mul(t2q[0:64, :], ps_b[1][:], c), inc=("vS", 1))
                qr_ev = pl.op("V", lambda e, o=qrb[:, fc * 512:(fc + 1) * 512]: e.tensor_add(o, t1q[:], t2q[0:64, :]),
                              inc=("vS", 1), waits=[("vS", rc)])
                st_state["pbv_free"][0] = st_state["pbv_free"][1] = ("vS", qr_ev)
            st_state["qr_ev"] = qr_ev
            c_wuv = dma_batch("G", "ga",
                              [(wuv_q[:, lc * 256:(lc + 1) * 256], wuv_d[lc * 128:(lc + 1) * 128, hq * 256:(hq + 1) * 256]) for lc in range(4)],
                              waits=[("tS", st_state["last_pv_t"])])
            msv = pl.op("V", lambda e: e.memset(Vq[:].rearrange("p (t w) -> p t w", w=65)[:, :, 64:65], 1.0),
                        inc=("vS", 1), waits=[("tS", st_state["last_pv_t"]), ("yc", ctab)])
            vq_ev = [("vS", msv), ("vS", msv)]
            for tc in range(32):
                rot = tc % 2
                tv = None
                for lc in range(4):
                    tv = pl.op("T", lambda e, w=ckv[:, lc * 4096 + tc * 128: lc * 4096 + (tc + 1) * 128],
                               r=wuv_q[:, lc * 256:(lc + 1) * 256],
                               o=ps_sc[rot][0:128, 0:256], st=(lc == 0), sp=(lc == 3):
                               e.matmul(o, w, r, start=st, stop=sp),
                               inc=("tS", 1) if lc == 3 else None,
                               waits=([("ga", c_wuv), vq_ev[rot], ("sS", st_state["last_exp_s"][rot])] if lc == 0 else []))
                oap = Vq[:, tc * 260:(tc + 1) * 260].rearrange("p (h c) -> p h c", h=4)[:, :, 0:64]
                iap = ps_sc[rot][0:128, 0:256].rearrange("p (h c) -> p h c", h=4)
                ev = pl.op("S", lambda e, i=iap, o=oap: e.copy(o, i), inc=("sS", 1), waits=[("tS", tv)])
                vq_ev[rot] = ("sS", ev)
                if filler and tc % 3 == 2:
                    filler.pop(0)()
            st_state["vq_ready"] = max(vq_ev[0][1], vq_ev[1][1])

            hb = hq * 4
            dq = dma_batch("G", "gq", [(qh[hb % 2][64:80, :], qra[0:16, :]),
                                       (qh[hb % 2][80:96, :], qrb[0:16, :])],
                           waits=[("vS", qr_ev), ("tS", st_state["qk_last"].get(hb - 2, 0))])
            st_state["bready"].setdefault(hb, {})["dq"] = dq
            st_state["dq_prev"] = dq
            while filler:
                filler.pop(0)()

        def attention_head(h, filler):
            par = h % 2
            hl2 = h % 4
            br = st_state["bready"][h]
            fin_stages = []
            for sg in range(4):
                sts = [2 * sg, 2 * sg + 1]
                Es = [4 * st + 4 for st in sts]

                gb = st_state["grp"] % 2
                st_state["grp"] += 1

                def emit_pv(todo, rot3, cv, gb=gb):
                    tv = None
                    for item in todo:
                        i_kc, kc, i_st, st = item
                        qi = i_kc * 2 + i_st
                        w0 = [("vS", cv)]
                        if kc == 0 and i_st == 0:
                            w0.append(st_state["last_avc"][gb])
                        tv = pl.op("T", lambda e, v=Vq[:, kc * 260 + hl2 * 65: kc * 260 + hl2 * 65 + 65],
                                   p=ptA[rot3][:, qi * 256:(qi + 1) * 256],
                                   o=ps_out[gb][0:65, i_st * 256:(i_st + 1) * 256],
                                   st=(kc == 0 and i_st == 0), sp=(i_st == 1 and kc == Es[1] - 1):
                                   e.matmul(o, v, p, start=st, stop=sp),
                                   inc=("tS", 1) if item == todo[-1] else None, waits=w0)
                    st_state["last_pv_for_pt"][rot3] = tv
                    return tv

                pend = []
                for kb in range(0, Es[1], 2):
                    rot = (kb // 2) % 2
                    rot3 = (kb // 2) % 3
                    todo = []
                    for i_kc, kc in enumerate((kb, kb + 1)):
                        for i_st, st in enumerate(sts):
                            if kc < Es[i_st]:
                                todo.append((i_kc, kc, i_st, st))
                    tv = None
                    first = True
                    for item in todo:
                        i_kc, kc, i_st, st = item
                        qi = i_kc * 2 + i_st
                        w0 = []
                        if first:
                            w0 = [("sS", st_state["last_exp_s"][rot]),
                                  br["q"], br["k"],
                                  ("gq", br.get("dq", 0)), ("sS", st_state["vq_ready"])]
                            first = False
                        tv = pl.op("T", lambda e, k=Kb[par][0:96, kc * 128:(kc + 1) * 128],
                                   q=qh[par][0:96, st * 256:(st + 1) * 256],
                                   o=ps_sc[rot][0:128, qi * 256:(qi + 1) * 256]:
                                   e.matmul(o, k, q, start=True, stop=True),
                                   inc=("tS", 1) if item == todo[-1] else None, waits=w0)
                    st_state["last_qk_t"] = tv
                    if fin_stages:
                        fin_stages.pop(0)()
                    if len(pend) >= 3:
                        emit_pv(*pend.pop(0))
                    sv = pl.op("S", lambda e, i=ps_sc[rot][:], o=ptA[rot3][:]:
                               e.activation(o, i, AF.Exp, scale=RS96),
                               inc=("sS", 1), waits=[("tS", tv), ("tS", st_state["last_pv_for_pt"][rot3])])
                    st_state["last_exp_s"][rot] = sv
                    cv = pl.op("V", lambda e, i=ptA[rot3][:]:
                               e.tensor_scalar(i, i, CLIP_HI, CLIP_LO, ALU.min, ALU.max),
                               inc=("vS", 1), waits=[("sS", sv)])
                    for (i_kc, kc, i_st, st) in todo:
                        qi = i_kc * 2 + i_st
                        m = kc - 4 * st
                        if 0 <= m <= 3:
                            cv = pl.op("V", lambda e, i=ptA[rot3][:, qi * 256:(qi + 1) * 256],
                                       mk=masks[:, m * 256:(m + 1) * 256]: e.tensor_mul(i, i, mk),
                                       inc=("vS", 1), waits=[("vS", cv)])
                    pend.append((todo, rot3, cv))
                    if filler:
                        filler.pop(0)()
                for p_ in pend:
                    st_state["last_pv_t"] = emit_pv(*p_)
                def mk_fin(sts=sts, hloc=h, gb=gb):
                    lpv = st_state["last_pv_t"]
                    stages = []

                    def s_copy():
                        c1 = pl.op("S", lambda e: e.copy(rrbf[:, 0:512], ps_out[gb][64:65, :]),
                                   inc=("sS", 1), waits=[("tS", lpv)])
                        av2 = pl.op("S", lambda e: e.copy(avbf[:, 0:512], ps_out[gb][0:64, :]),
                                    inc=("sS", 1), waits=[("vS", st_state["last_fin_v"])])
                        st_state["fin_av2"] = av2
                    stages.append(s_copy)

                    def s_tb():
                        tb = pl.op("T", lambda e: e.matmul(ps_out[gb][0:64, :], ones1[64:65, 0:64], rrbf,
                                                           start=True, stop=True),
                                   inc=("tS", 1), waits=[("sS", st_state["fin_av2"])])
                        st_state["fin_tb"] = tb
                    stages.append(s_tb)

                    def mk_rc(c):
                        def s_rc():
                            cr = pl.op("V", lambda e, i=ps_out[gb][0:64, c * 64:(c + 1) * 64],
                                       o=rcp_sb[:, c * 64:(c + 1) * 64]: e.reciprocal(o, i),
                                       inc=("vS", 1),
                                       waits=([("tS", st_state["fin_tb"]), ("vS", st_state["last_fin_v"])] if c == 0 else []))
                            st_state["fin_cr"] = cr
                        return s_rc
                    for c in range(8):
                        stages.append(mk_rc(c))

                    def s_mul():
                        fin = pl.op("V", lambda e, a=avbf[:, 0:512], r=rcp_sb[:, 0:512],
                                    o=attn_cc[(hloc % 2) * 64:(hloc % 2) * 64 + 64,
                                              (hloc // 2) * 2048 + sts[0] * 256:(hloc // 2) * 2048 + (sts[0] + 2) * 256]:
                                    e.tensor_mul(o, a, r),
                                    inc=("vS", 1), waits=[("vS", st_state["fin_cr"]), ("sS", st_state["fin_av2"])])
                        st_state["last_fin_v"] = fin
                        st_state["last_avc"][gb] = ("vS", fin)
                    stages.append(s_mul)
                    return stages

                if sg < 3:
                    fin_stages.extend(mk_fin())
                else:
                    for s_ in mk_fin():
                        s_()
            assert not fin_stages, f"fin stages left at head {h}: {len(fin_stages)}"
            st_state["qk_last"][h] = st_state["last_qk_t"]
            while filler:
                filler.pop(0)()

        load_quarter_weights(0)
        quarter_start(0, filler=build_quanta(0))
        for h in range(16):
            if h % 4 == 0 and h > 0:
                load_quarter_weights(h // 4) if (h // 4) not in st_state["wq_loads"] else None
                quarter_start(h // 4)
            if (h + 1) // 4 != h // 4 and h + 1 < 16 and (h + 1) // 4 not in st_state["wq_loads"]:
                load_quarter_weights((h + 1) // 4)
            filler = build_quanta(h + 1) if h + 1 < 16 else []
            attention_head(h, filler)
            if debug and h == 0:
                dbg_dump("dbg_qh", qh[0][:])
                dbg_dump("dbg_k", Kb[0][:])
                dbg_dump("dbg_v", Vq[:])

        dbg_dump("dbg_attn", attn_cc[:])
        pl.barrier()
    kv_st.close()

    acc_st = ExitStack()
    acc = acc_st.enter_context(nc.sbuf_tensor("acc", [128, 16 * 1024], BF16, side="right"))
    w1buf_st = ExitStack()
    w1buf = w1buf_st.enter_context(nc.sbuf_tensor("w1buf", [128, 8 * 2048], BF16, side="right"))
    with ExitStack() as P4:
        wo = P4.enter_context(nc.sbuf_tensor("wo_s", [128, 8 * 1024], BF16, side="right"))
        ps_wo = [P4.enter_context(nc.psum_tensor(f"ps_wo{i}", [128, 1024], F32)) for i in range(2)]
        c_wo = dma_batch("G", "ga", [(wo[:, oc * 1024:(oc + 1) * 1024], wo_d[oc * 128:(oc + 1) * 128, :]) for oc in range(8)])
        w1_loads = {(0, 0): dma_batch("G", "gw1", [(w1buf[:, dc * 2048:(dc + 1) * 2048],
                                                    w1_d[0, dc * 128:(dc + 1) * 128, 0:2048]) for dc in range(8)])}
        ev_free = [0, 0]
        for tj in range(NT):
            rot = tj % 2
            tv = None
            for dh in range(2):
                for oc in range(8):
                    pl.op("T", lambda e, a=attn_cc[:, oc * 2048 + tj * 128: oc * 2048 + (tj + 1) * 128],
                          w=wo[:, oc * 1024 + dh * 512: oc * 1024 + (dh + 1) * 512],
                          o=ps_wo[rot][:, dh * 512:(dh + 1) * 512], st=(oc == 0):
                          e.matmul(o, a, w, start=st, stop=False),
                          waits=([("ga", c_wo), ("vS", ev_free[rot])] if (oc == 0 and dh == 0) else []))
                tv = pl.op("T", lambda e, g=g_T[0:8, tj * 128:(tj + 1) * 128],
                           b=b2_sb[0:8, dh * 512:(dh + 1) * 512],
                           o=ps_wo[rot][:, dh * 512:(dh + 1) * 512]:
                           e.matmul(o, g, b, start=False, stop=True),
                           inc=("tS", 1) if dh == 1 else None)
            ev = pl.op("V", lambda e, o=acc[:, tj * 1024:(tj + 1) * 1024], i=ps_wo[rot][:]: e.tensor_copy(o, i),
                       inc=("vS", 1), waits=[("tS", tv)])
            ev_free[rot] = ev
        dbg_dump("dbg_acc4", acc[:])
        pl.barrier()
    cc_st.close()

    with ExitStack() as P5:
        h_store = P5.enter_context(nc.sbuf_tensor("h_store", [128, 16 * 2048], BF16, side="right"))
        w2buf = P5.enter_context(nc.sbuf_tensor("w2buf", [128, 16 * 1024], BF16, side="right"))
        ps_h = [P5.enter_context(nc.psum_tensor(f"ps_h{i}", [128, 1024], F32)) for i in range(2)]
        ps_o = [P5.enter_context(nc.psum_tensor(f"ps_o{i}", [128, 1024], F32)) for i in range(2)]

        xr = [P5.enter_context(nc.sbuf_tensor(f"xr{i}", [128, 1024], F32, side="right")) for i in range(2)]
        add_v = [0, 0]
        st_c = [0, 0]
        gelu_free = [0, 0]
        stt_free = [0, 0]
        w2_loads = {}
        consume_done_t = 0
        steps = [(e_i, hf) for e_i in range(E) for hf in range(2)]
        for idx, (e_i, hf) in enumerate(steps):
            w2_loads[(e_i, hf)] = dma_batch("G", "gw2",
                                            [(w2buf[:, c * 1024:(c + 1) * 1024],
                                              w2_d[e_i, (hf * 16 + c) * 128:(hf * 16 + c + 1) * 128, :]) for c in range(16)],
                                            waits=[("tS", consume_done_t)])
            gel = 0
            if idx == 0:
                pl.op("S", None, waits=[("yc", pl.cnt["yc"])])
            for c in range(16):
                for th in range(2):
                    par = (c * 2 + th) % 2
                    tv = None
                    for sub in range(2):
                        for dc in range(8):
                            w0 = []
                            if sub == 0 and dc == 0:
                                w0 = [("sS", gelu_free[par]), ("gw1", w1_loads[(e_i, hf)])]
                            tv = pl.op("T", lambda e, w=w1buf[:, dc * 2048 + c * 128: dc * 2048 + (c + 1) * 128],
                                       x=xn_own[:, dc * 2048 + th * 1024 + sub * 512: dc * 2048 + th * 1024 + (sub + 1) * 512],
                                       o=ps_h[par][:, sub * 512:(sub + 1) * 512], st=(dc == 0), sp=(dc == 7):
                                       e.matmul(o, w, x, start=st, stop=sp),
                                       inc=("tS", 1) if (sub == 1 and dc == 7) else None, waits=w0)
                    gel = pl.op("S", lambda e, i=ps_h[par][:],
                                o=h_store[:, c * 2048 + th * 1024: c * 2048 + (th + 1) * 1024],
                                b=b1_sb[:, e_i * 32 + hf * 16 + c: e_i * 32 + hf * 16 + c + 1]:
                                e.activation(o, i, AF.Gelu, bias=b),
                                inc=("sS", 1), waits=[("tS", tv)])
                    gelu_free[par] = gel
            produce_done_t = pl.cnt["tS"]
            if idx + 1 < len(steps):
                e2, h2 = steps[idx + 1]
                w1_loads[(e2, h2)] = dma_batch("G", "gw1",
                                               [(w1buf[:, dc * 2048:(dc + 1) * 2048],
                                                 w1_d[e2, dc * 128:(dc + 1) * 128, h2 * 2048:(h2 + 1) * 2048]) for dc in range(8)],
                                               waits=[("tS", produce_done_t)])
            for tj in range(NT):
                rot = tj % 2
                ldp = None
                if idx == len(steps) - 1:
                    xsem, ssem = "yx%d" % rot, "ys%d" % rot
                    ldp = dma_batch("Y", xsem, [(xr[rot][:], xo_d[tj * 128:(tj + 1) * 128, :])],
                                    waits=[("vS", add_v[rot]), (ssem, st_c[rot])])
                tv = None
                for c in range(16):
                    for dh in range(2):
                        w0 = []
                        if c == 0 and dh == 0:
                            w0 = [("sS", gel), ("gw2", w2_loads[(e_i, hf)]), ("vS", stt_free[rot])]
                        tv = pl.op("T", lambda e, hh=h_store[:, c * 2048 + tj * 128: c * 2048 + (tj + 1) * 128],
                                   w=w2buf[:, c * 1024 + dh * 512: c * 1024 + (dh + 1) * 512],
                                   o=ps_o[rot][:, dh * 512:(dh + 1) * 512], st=(c == 0), sp=(c == 15):
                                   e.matmul(o, hh, w, start=st, stop=sp),
                                   inc=("tS", 1) if (c == 15 and dh == 1) else None, waits=w0)
                sv = pl.op("V", lambda e, p=ps_o[rot][:], a=acc[:, tj * 1024:(tj + 1) * 1024],
                           g=g_sb[:, tj * 8 + e_i: tj * 8 + e_i + 1]:
                           e.scalar_tensor_tensor(a, p, g, a, ALU.mult, ALU.add),
                           inc=("vS", 1), waits=[("tS", tv)])
                stt_free[rot] = sv
                if idx == len(steps) - 1:
                    xsem, ssem = "yx%d" % rot, "ys%d" % rot
                    av = pl.op("V", lambda e, x=xr[rot][:], a=acc[:, tj * 1024:(tj + 1) * 1024]:
                               e.tensor_add(x, x, a), inc=("vS", 1),
                               waits=[(xsem, ldp), ("vS", sv)])
                    add_v[rot] = av
                    st_c[rot] = dma_batch("Y", ssem, [(out_d[tj * 128:(tj + 1) * 128, :], xr[rot][:])],
                                          waits=[("vS", av)])
            consume_done_t = pl.cnt["tS"]
        dbg_dump("dbg_acc5", acc[:])
        pl.barrier()
    w1buf_st.close()

    for s in ("ys0", "ys1", "yc", "yx0", "yx1"):
        pl.op("Y", None, waits=[(s, pl.cnt[s])])
    acc_st.close()
    glb.close()

    with ExitStack() as R:
        sems = {s: R.enter_context(nc.semaphore(s)) for s in ("bar", "tS", "vS", "sS") + DMA_SEMS}
        blk = R.enter_context(nc.Block())
        pl.render(blk, sems)
    return nc



def _host_arrays(inputs):
    f32 = np.float32
    C = np.ascontiguousarray
    W_DQ = np.asarray(inputs["L0_attn_W_DQ"], f32)
    W_UQ = np.asarray(inputs["L0_attn_W_UQ"], f32)
    W_QR = np.asarray(inputs["L0_attn_W_QR"], f32)
    W_DKV = np.asarray(inputs["L0_attn_W_DKV"], f32)
    W_UK = np.asarray(inputs["L0_attn_W_UK"], f32)
    W_UV = np.asarray(inputs["L0_attn_W_UV"], f32)
    W_KR = np.asarray(inputs["L0_attn_W_KR"], f32)
    W_O = np.asarray(inputs["L0_attn_W_O"], f32)

    pq = []
    for h in range(H):
        pq += [h * 32 + 2 * f for f in range(16)]
    for h in range(H):
        pq += [h * 32 + 2 * f + 1 for f in range(16)]
    pq = np.array(pq)
    p32 = np.array([2 * f for f in range(16)] + [2 * f + 1 for f in range(16)])

    shared = {
        "wdq_t": C(W_DQ.T), "wdkv_t": C(W_DKV.T), "wkr_t": C(W_KR[p32, :].T),
        "wuq_t": C(W_UQ.T), "wqr_t": C(W_QR[pq, :].T), "wuk_t": C(W_UK.T),
        "wuv_t": C(W_UV.T), "wo_t": C(W_O.T),
        "wg": C(np.asarray(inputs["L0_ffn_Wg"], f32)),
        "bg2": C(np.asarray(inputs["L0_ffn_bg"], f32).reshape(E, 1)),
        "b2r": C(np.asarray(inputs["L0_ffn_b2"], f32)),
        "w1": C(np.asarray(inputs["L0_ffn_W1"], f32)),
        "w2": C(np.asarray(inputs["L0_ffn_W2"], f32)),
        "i128": np.eye(128, dtype=ml_dtypes.bfloat16),
        "ones1": np.ones((65, 128), dtype=ml_dtypes.bfloat16),
    }
    b1 = np.asarray(inputs["L0_ffn_b1"], f32)
    b1r = np.empty((128, E * 32), f32)
    for e in range(E):
        b1r[:, e * 32:(e + 1) * 32] = b1[e].reshape(32, 128).T
    shared["b1r"] = C(b1r)

    inv = 10000.0 ** (-np.arange(0, 32, 2, dtype=np.float64) / 32.0)
    ang = inv[:, None] * np.arange(S, dtype=np.float64)[None, :]
    shared["cosk"] = np.cos(ang).astype(ml_dtypes.bfloat16)
    shared["sink"] = np.sin(ang).astype(ml_dtypes.bfloat16)
    return shared, inv


def _per_core(inputs, core, inv):
    f32 = np.float32
    x = np.asarray(inputs["L0_attn_x"], f32)
    b, p = core // 2, core % 2
    tiles = [2 * j + p for j in range(NT)]
    xb = x[b]
    xo = np.ascontiguousarray(xb.reshape(32, 128, D)[tiles].reshape(NTOK, D))
    pos_own = (np.array(tiles)[:, None] * 128 + np.arange(128)[None, :]).reshape(-1).astype(np.float64)
    ang = inv[:, None] * pos_own[None, :]
    cosq = np.tile(np.cos(ang), (8, 1)).astype(ml_dtypes.bfloat16)
    sinq = np.tile(np.sin(ang), (8, 1)).astype(ml_dtypes.bfloat16)
    ONESM = np.ones((128, 128), f32)
    ZEROM = np.zeros((128, 128), f32)
    DIAG = np.triu(np.ones((128, 128), f32))

    def pat(m, off):
        t = off + p
        return ONESM if m < t else (DIAG if m == t else ZEROM)

    masks = np.concatenate([np.concatenate([pat(m, 0), pat(m, 2)], axis=1) for m in range(4)], axis=1)
    return {
        "xf": np.ascontiguousarray(xb),
        "xo": xo,
        "cosq": cosq,
        "sinq": sinq,
        "masks": masks.astype(ml_dtypes.bfloat16),
    }


_NC_CACHE = {}


def kernel(**inputs):
    shared, inv = _host_arrays(inputs)
    in_maps = []
    for c in range(8):
        m = dict(shared)
        m.update(_per_core(inputs, c, inv))
        in_maps.append(m)
    if "main" not in _NC_CACHE:
        _NC_CACHE["main"] = build_program(debug=False)
    nc = _NC_CACHE["main"]
    res = run_bass_kernel_spmd(nc, in_maps, core_ids=list(range(8)))
    out = np.empty((B, S, D), np.float32)
    for c in range(8):
        b, p = c // 2, c % 2
        tiles = [2 * j + p for j in range(NT)]
        out[b].reshape(32, 128, D)[tiles] = res.results[c]["out"].reshape(NT, 128, D)
    return out
